# version 12
# baseline (speedup 1.0000x reference)
import numpy as np
import ml_dtypes
from contextlib import ExitStack
import concourse.bass as bass
import concourse.mybir as mybir
from concourse.bass_utils import run_bass_kernel_spmd

F32 = mybir.dt.float32
BF16 = mybir.dt.bfloat16
AF = mybir.ActivationFunctionType
ALU = mybir.AluOpType
AX = mybir.AxisListType

L = 2048
DM = 2048
EM = 2048
FF = 1024
NEXP = 8
TL = 16
TD = 16
TEm = 16
TFm = 8
KW3 = 8
EPS = 1e-5
ENGS = ("sync", "gpsimd", "tensor", "vector", "scalar")

_CACHED = {}


def build():
    nc = bass.Bass("TRN2", target_bir_lowering=False)

    xin = nc.dram_tensor("x", [L, DM], F32, kind="ExternalInput").ap()
    wi_d = nc.dram_tensor("wi", [EM, DM], F32, kind="ExternalInput").ap()
    wf_d = nc.dram_tensor("wf", [EM, DM], F32, kind="ExternalInput").ap()
    wg_d = nc.dram_tensor("wg", [EM, DM], F32, kind="ExternalInput").ap()
    gw_d = nc.dram_tensor("gwv", [EM], F32, kind="ExternalInput").ap()
    wo_d = nc.dram_tensor("wo", [DM, EM], F32, kind="ExternalInput").ap()
    rw_d = nc.dram_tensor("rw", [NEXP, DM], F32, kind="ExternalInput").ap()
    w1_d = nc.dram_tensor("w1", [NEXP, DM, FF], F32, kind="ExternalInput").ap()
    w2_d = nc.dram_tensor("w2", [NEXP, DM, FF], F32, kind="ExternalInput").ap()
    w3_d = nc.dram_tensor("w3", [NEXP, FF, DM], F32, kind="ExternalInput").ap()
    yout = nc.dram_tensor("y", [L, DM], F32, kind="ExternalOutput").ap()

    ident_dram = nc.inline_tensor(np.eye(128, dtype=ml_dtypes.bfloat16), name="identc")
    identf8_dram = nc.inline_tensor(np.eye(8, dtype=np.float32), name="identf8c")
    sel_np = np.zeros((NEXP, NEXP * 128), dtype=ml_dtypes.bfloat16)
    for _e in range(NEXP):
        sel_np[_e, _e * 128:(_e + 1) * 128] = 1.0
    sel_dram = nc.inline_tensor(sel_np, name="selc")

    plan = {e: [] for e in ENGS}
    waited = {e: {} for e in ENGS}

    def on(eng, f):
        plan[eng].append(f)

    with ExitStack() as ctx:
        class S:
            __slots__ = ("h", "n", "name")

            def __init__(self, name):
                self.h = ctx.enter_context(nc.semaphore(name))
                self.n = 0
                self.name = name

        te = S("te")
        ve = S("ve")
        ac = S("ac")
        miscs = S("miscs")
        miscg = S("miscg")

        def wait(eng, s, val=None):
            v = s.n if val is None else val
            if v is None or v <= 0:
                return
            if waited[eng].get(s.name, 0) >= v:
                return
            waited[eng][s.name] = v
            on(eng, lambda e, _h=s.h, _v=v: e.wait_ge(_h, _v))

        def emit(eng, f, inc=None, n=1):
            if eng == "scalar":
                wait(eng, ac, ac.n)
                if inc is None:
                    inc = ac
            elif eng == "vector":
                wait(eng, ve, ve.n)
                if inc is None:
                    inc = ve
            if inc is None:
                on(eng, f)
                return None
            inc.n += n
            on(eng, lambda e, _f=f, _h=inc.h, _n=n: _f(e).then_inc(_h, _n))
            return inc.n

        class Stream:

            def __init__(self, name, nslots, queue):
                self.q = queue
                self.sems = [S(f"{name}{i}") for i in range(nslots)]
                self.rel = [None] * nslots
                self.k = 0

            def load(self, fdma):
                i = self.k % len(self.sems)
                s = self.sems[i]
                if self.rel[i]:
                    for (rs, rv) in self.rel[i]:
                        wait(self.q, rs, rv)
                    self.rel[i] = None
                s.n += 16
                on(self.q, lambda e, _f=fdma, _h=s.h: _f(e).then_inc(_h, 16))
                self.k += 1
                return (i, s, s.n)

            def release(self, slot, events):
                self.rel[slot[0]] = events

        def use(eng, slot):
            wait(eng, slot[1], slot[2])

        sb = lambda n_, s_, d_: ctx.enter_context(nc.sbuf_tensor(n_, s_, d_))
        bufA = sb("bufA", [128, TL * L], BF16)
        bufB = sb("bufB", [128, TL * L], BF16)
        gvw = sb("gvw", [128, 8 * L], BF16)
        wsc = sb("wsc", [128, 10 * 1024], BF16)
        fsc = sb("fsc", [128, 2 * 2048], F32)
        scr2 = sb("scr2", [128, 1024], BF16)
        ident = sb("identsb", [128, 128], BF16)
        identf8 = sb("identf8sb", [8, 8], F32)
        gwt = sb("gwt", [128, 16], F32)
        stats = sb("stats", [128, 256], F32)
        routerT = sb("routerTsb", [128, 128], BF16)
        logtm = sb("logtm", [128, 128], F32)
        top8s = sb("top8s", [128, 128], F32)
        wdense = sb("wdense", [128, 128], BF16)
        wfm = sb("wfm", [8, L], BF16)
        tmp8a = sb("tmp8a", [128, 8], F32)
        tmp8b = sb("tmp8b", [128, 8], F32)
        onesbf = sb("onesbf", [1, 128], BF16)
        selsb = sb("selsb", [8, NEXP * 128], BF16)

        rem = nc.sbuf_bytes_remaining
        assert rem >= 0, f"SBUF over budget by {-rem} bytes"

        fs = [fsc[:, i * 2048:(i + 1) * 2048] for i in range(2)]
        xs_ap = [fs[0], fs[1]]
        bGate, bX = fs[0], fs[1]
        wb_ap = fs[0]
        sg_ap = fsc[:, 2048:2048 + 1024]
        dummy_bf = bufB[:, 0:2048]

        SUM, SQ, MEAN, VAR, INV, LNB = 0, 16, 32, 48, 64, 80
        SS, RMS, T0, T1 = 96, 112, 128, 144
        DALL, WAcol, WBcol = 160, 176, 192

        st_c = lambda c0: stats[:, c0:c0 + 16]
        st1 = lambda c0, t: stats[:, c0 + t:c0 + t + 1]

        xs = Stream("xs", 2, "sync")
        ypre = S("ypre")
        xq0 = S("xq0")
        xq1 = S("xq1")
        spill = Stream("sp", 2, "gpsimd")
        accs = Stream("oa", 4, "gpsimd")
        wgs = Stream("wgs", 2, "gpsimd")
        wis = Stream("wis", 2, "gpsimd")
        wfs = Stream("wfs", 2, "gpsimd")
        wc1 = Stream("wc1", 2, "gpsimd")
        wc2 = Stream("wc2", 2, "gpsimd")
        wc3 = Stream("wc3", 2, "gpsimd")

        def misc_load(queue, fdma):
            ms = miscs if queue == "sync" else miscg
            ms.n += 16
            on(queue, lambda e, _f=fdma, _h=ms.h: _f(e).then_inc(_h, 16))
            return ms.n

        misc_load("sync", lambda e: e.dma_start(out=ident[:], in_=ident_dram.ap()))
        misc_load("sync", lambda e: e.dma_start(out=identf8[:], in_=identf8_dram.ap()))
        misc_load("gpsimd", lambda e: e.dma_start(
            out=gwt[:], in_=gw_d.rearrange("(m p) -> p m", p=128),
            allow_slow_non_contiguous=True))
        misc_load("sync", lambda e: e.dma_start(out=selsb[:], in_=sel_dram.ap()))
        MISC_S = miscs.n
        MISC_G = miscg.n

        ev_ones = emit("vector", lambda e: e.memset(onesbf[:], 1.0), inc=ve)

        for t in range(TL):
            slot = xs.load(lambda e, _t=t, _i=xs.k % 2: e.dma_start(
                out=xs_ap[_i], in_=xin[_t * 128:(_t + 1) * 128, :]))
            use("scalar", slot)
            i = slot[0]
            emit("scalar", lambda e, _i=i, _t=t: e.activation(
                out=bufB[:, _t * 2048:(_t + 1) * 2048], in_=xs_ap[_i], func=AF.Copy,
                accum_out=st1(SUM, _t)), inc=ac)
            use("vector", slot)
            v = emit("vector", lambda e, _i=i, _t=t: e.scalar_tensor_tensor(
                out=gvw[:, 0:2048], in0=xs_ap[_i], scalar=1.0, in1=xs_ap[_i],
                op0=ALU.mult, op1=ALU.mult, accum_out=st1(SQ, _t)), inc=ve)
            xs.release(slot, [(ve, v), (ac, ac.n)])
        AC_STAGE1 = ac.n
        VE_STAGE1 = ve.n
        ypre.n += 16
        on("sync", lambda e, _h=ypre.h: e.dma_start(out=yout, in_=xin).then_inc(_h, 16))

        def batch_invstd():
            wait("vector", ac)
            emit("vector", lambda e: e.tensor_scalar_mul(st_c(MEAN), st_c(SUM), 1.0 / DM))
            emit("vector", lambda e: e.tensor_scalar_mul(st_c(VAR), st_c(SQ), 1.0 / DM))
            emit("vector", lambda e: e.tensor_mul(st_c(T0), st_c(MEAN), st_c(MEAN)))
            emit("vector", lambda e: e.tensor_sub(st_c(VAR), st_c(VAR), st_c(T0)))
            v0 = emit("vector", lambda e: e.tensor_scalar_add(st_c(VAR), st_c(VAR), EPS), inc=ve)
            wait("scalar", ve, v0)
            emit("scalar", lambda e: e.activation(out=st_c(T0), in_=st_c(VAR), func=AF.Ln))
            a0 = emit("scalar", lambda e: e.activation(out=st_c(INV), in_=st_c(T0), func=AF.Exp,
                                                       scale=-0.5), inc=ac)
            wait("vector", ac, a0)
            emit("vector", lambda e: e.tensor_mul(st_c(T1), st_c(INV), st_c(INV)))
            emit("vector", lambda e: e.tensor_mul(st_c(T1), st_c(T1), st_c(VAR)))
            emit("vector", lambda e: e.tensor_scalar(st_c(T1), st_c(T1), -0.5, 1.5, ALU.mult, ALU.add))
            emit("vector", lambda e: e.tensor_mul(st_c(INV), st_c(INV), st_c(T1)))
            emit("vector", lambda e: e.tensor_mul(st_c(LNB), st_c(MEAN), st_c(INV)))
            return emit("vector", lambda e: e.tensor_scalar_mul(st_c(LNB), st_c(LNB), -1.0), inc=ve)

        v_inv1 = batch_invstd()

        wait("vector", ve, v_inv1)
        wait("vector", ac, AC_STAGE1)
        wait("vector", ve, VE_STAGE1)
        for t in range(TL):
            emit("vector", lambda e, _t=t: e.tensor_scalar(
                bufB[:, _t * 2048:(_t + 1) * 2048], bufB[:, _t * 2048:(_t + 1) * 2048],
                st1(INV, _t), st1(LNB, _t), ALU.mult, ALU.add), inc=ve)
        VE_APPLY1 = ve.n

        def transpose_tm_to_fm(src_buf, dst_buf, slabs):
            slab_rel = [None, None]
            for j in range(TD):
                sl = j % 2
                if slab_rel[sl] is not None:
                    wait("tensor", ve, slab_rel[sl])

                def tgrp(e, _j=j, _sl=sl):
                    ins = None
                    for t_ in range(TL):
                        ins = e.transpose(out=slabs[_sl][:, t_ * 128:(t_ + 1) * 128],
                                          in_=src_buf[:, t_ * 2048 + _j * 128:
                                                      t_ * 2048 + _j * 128 + 128],
                                          identity=ident[:])
                    return ins
                vte = emit("tensor", tgrp, inc=te)
                wait("vector", te, vte)
                slab_rel[sl] = emit("vector", lambda e, _j=j, _sl=sl: e.tensor_copy(
                    out=dst_buf[:, _j * 2048:(_j + 1) * 2048], in_=slabs[_sl][:]), inc=ve)

        wait("tensor", ve, ve.n)
        wait("tensor", ac, ac.n)
        with nc.psum_tensor("slA0", [128, 2048], BF16) as slA0, \
             nc.psum_tensor("slA1", [128, 2048], BF16) as slA1:
            wait("tensor", miscs, MISC_S)
            wait("tensor", ve, VE_APPLY1)
            transpose_tm_to_fm(bufB, bufA, [slA0, slA1])
        TE_A = te.n
        VE_A = ve.n

        natsl = [wsc[:, 0:2048], wsc[:, 2048:4096]]
        tsl = [wsc[:, 4096:6144], wsc[:, 6144:8192]]
        gsqb = [gvw[:, 0:2048], gvw[:, 2048:4096]]

        def rowblock_into(stream, dram, m, dst):
            return stream.load(lambda e, _m=m, _d=dst: e.dma_start(
                out=_d, in_=dram[_m * 128:(_m + 1) * 128, :]))

        wait("tensor", ve, ve.n)
        wait("tensor", ac, ac.n)
        with nc.psum_tensor("pg0", [128, 1024], F32) as pg0, \
             nc.psum_tensor("pg1", [128, 1024], F32) as pg1, \
             nc.psum_tensor("slW", [128, 2048], BF16) as slW, \
             nc.psum_tensor("slQ", [128, 2048], BF16) as slQ:
            pgs = [pg0, pg1]
            pg_rel = [None, None]
            slW_rel = None
            slQ_rel = None
            sq_ev = {}

            wg_slots = {0: rowblock_into(wgs, wg_d, 0, natsl[0])}
            copy_ev = {}
            mm_last = {}

            def wg_prep(m):
                slot = wg_slots[m]
                use("tensor", slot)
                nonlocal slW_rel
                if slW_rel is not None:
                    wait("tensor", ve, slW_rel)

                def wtr(e, _i=slot[0]):
                    ins = None
                    for k in range(TD):
                        ins = e.transpose(out=slW[:, k * 128:(k + 1) * 128],
                                          in_=natsl[_i][:, k * 128:(k + 1) * 128],
                                          identity=ident[:])
                    return ins
                vte = emit("tensor", wtr, inc=te)
                wait("vector", te, vte)
                if m >= 2 and (m - 2) in mm_last:
                    wait("vector", te, mm_last[m - 2])
                slW_rel = emit("vector", lambda e, _m=m: e.tensor_copy(
                    out=tsl[_m % 2][:], in_=slW[:]), inc=ve)
                copy_ev[m] = slW_rel
                wgs.release(slot, [(te, vte)])

            wg_prep(0)
            for m in range(TEm):
                if m + 1 < TEm:
                    wg_slots[m + 1] = rowblock_into(wgs, wg_d, m + 1, natsl[(m + 1) % 2])
                wait("tensor", ve, copy_ev[m])
                for h in range(2):
                    if pg_rel[h] is not None:
                        wait("tensor", ac, pg_rel[h])

                    def gmm(e, _m=m, _h=h):
                        ins = None
                        for k in range(TD):
                            lhsT = tsl[_m % 2][:, k * 128:(k + 1) * 128]
                            for n2 in range(2):
                                nn = _h * 1024 + n2 * 512
                                ins = e.matmul(out=pgs[_h][:, n2 * 512:(n2 + 1) * 512],
                                               lhsT=lhsT,
                                               rhs=bufA[:, k * 2048 + nn:k * 2048 + nn + 512],
                                               start=(k == 0), stop=(k == TD - 1))
                        return ins
                    vmm = emit("tensor", gmm, inc=te)
                    mm_last[m] = vmm
                    wait("scalar", te, vmm)
                    wait("scalar", miscg, MISC_G)
                    emit("scalar", lambda e, _m=m, _h=h: e.activation(
                        out=bufB[:, _m * 2048 + _h * 1024:_m * 2048 + _h * 1024 + 1024],
                        in_=pgs[_h][:], func=AF.Copy, scale=gwt[:, _m:_m + 1]), inc=ac)
                    pg_rel[h] = emit("scalar", lambda e, _m=m, _h=h: e.activation(
                        out=gsqb[_m % 2][:, _h * 1024:_h * 1024 + 1024],
                        in_=pgs[_h][:], func=AF.Square), inc=ac)
                    if h == 0 and m + 1 < TEm:
                        wg_prep(m + 1)

                for mq in ([m - 1] if m >= 1 else []) + ([m] if m == TEm - 1 else []):
                    wait("tensor", ac, pg_rel[1] if mq == m else None) if mq == m else None
                    if mq == m:
                        wait("tensor", ac, pg_rel[1])
                    else:
                        wait("tensor", ac, sq_ev[mq])
                    if slQ_rel is not None:
                        wait("tensor", ve, slQ_rel)

                    def qtr(e, _m=mq):
                        ins = None
                        for t_ in range(TL):
                            ins = e.transpose(out=slQ[:, t_ * 128:(t_ + 1) * 128],
                                              in_=gsqb[_m % 2][:, t_ * 128:(t_ + 1) * 128],
                                              identity=ident[:])
                        return ins
                    vq = emit("tensor", qtr, inc=te)
                    wait("vector", te, vq)
                    if mq == 0:
                        slQ_rel = emit("vector", lambda e: e.tensor_reduce(
                            out=st_c(SS), in_=slQ[:].rearrange("p (t c) -> p t c", t=TL),
                            axis=AX.X, op=ALU.add), inc=ve)
                    else:
                        emit("vector", lambda e: e.tensor_reduce(
                            out=st_c(T0), in_=slQ[:].rearrange("p (t c) -> p t c", t=TL),
                            axis=AX.X, op=ALU.add))
                        slQ_rel = emit("vector", lambda e: e.tensor_add(
                            st_c(SS), st_c(SS), st_c(T0)), inc=ve)
                sq_ev[m] = pg_rel[1]

            v0 = emit("vector", lambda e: e.tensor_scalar(
                st_c(VAR), st_c(SS), 1.0 / EM, EPS, ALU.mult, ALU.add), inc=ve)
            wait("scalar", ve, v0)
            emit("scalar", lambda e: e.activation(out=st_c(T0), in_=st_c(VAR), func=AF.Ln))
            a0 = emit("scalar", lambda e: e.activation(out=st_c(RMS), in_=st_c(T0), func=AF.Exp,
                                                       scale=-0.5), inc=ac)
            wait("vector", ac, a0)
            emit("vector", lambda e: e.tensor_mul(st_c(T1), st_c(RMS), st_c(RMS)))
            emit("vector", lambda e: e.tensor_mul(st_c(T1), st_c(T1), st_c(VAR)))
            emit("vector", lambda e: e.tensor_scalar(st_c(T1), st_c(T1), -0.5, 1.5, ALU.mult, ALU.add))
            v_rms = emit("vector", lambda e: e.tensor_mul(st_c(RMS), st_c(RMS), st_c(T1)), inc=ve)
        TE_B0 = te.n

        winat = [gvw[:, 0:2048], gvw[:, 2048:4096]]
        wfnat = [gvw[:, 4096:6144], gvw[:, 6144:8192]]
        wiT = [gvw[:, 8192:10240], gvw[:, 10240:12288]]
        wfT = [gvw[:, 12288:14336], gvw[:, 14336:16384]]

        wait("gpsimd", te, TE_B0)
        wo_slots = {0: rowblock_into(wgs, wo_d, 0, natsl[0]),
                    1: rowblock_into(wgs, wo_d, 1, natsl[1])}

        wait("tensor", ve, ve.n)
        wait("tensor", ac, ac.n)
        with nc.psum_tensor("pi", [128, 1024], F32) as pi, \
             nc.psum_tensor("pf", [128, 1024], F32) as pf, \
             nc.psum_tensor("slWi", [128, 2048], BF16) as slWi, \
             nc.psum_tensor("slWf", [128, 2048], BF16) as slWf:
            slWi_rel = None
            slWf_rel = None
            pi_rel = None
            pf_rel = None
            wi_slots = {0: rowblock_into(wis, wi_d, 0, winat[0])}
            wf_slots = {0: rowblock_into(wfs, wf_d, 0, wfnat[0])}
            copy_evB = {}
            mm_lastB = {}
            u_ev = None

            def if_prep(m):
                nonlocal slWi_rel, slWf_rel
                for which in ("i", "f"):
                    slot = (wi_slots if which == "i" else wf_slots)[m]
                    nat = winat if which == "i" else wfnat
                    slab = slWi if which == "i" else slWf
                    tdst = wiT if which == "i" else wfT
                    rel = slWi_rel if which == "i" else slWf_rel
                    use("tensor", slot)
                    if rel is not None:
                        wait("tensor", ve, rel)

                    def wtr(e, _i=slot[0], _nat=nat, _slab=slab):
                        ins = None
                        for k in range(TD):
                            ins = e.transpose(out=_slab[:, k * 128:(k + 1) * 128],
                                              in_=_nat[_i][:, k * 128:(k + 1) * 128],
                                              identity=ident[:])
                        return ins
                    vte = emit("tensor", wtr, inc=te)
                    wait("vector", te, vte)
                    if m >= 2 and (m - 2) in mm_lastB:
                        wait("vector", te, mm_lastB[m - 2])
                    nrel = emit("vector", lambda e, _m=m, _slab=slab, _tdst=tdst: e.tensor_copy(
                        out=_tdst[_m % 2][:], in_=_slab[:]), inc=ve)
                    if which == "i":
                        slWi_rel = nrel
                    else:
                        slWf_rel = nrel
                    (wis if which == "i" else wfs).release(slot, [(te, vte)])
                copy_evB[m] = nrel

            if_prep(0)
            for m in range(TEm):
                if m + 1 < TEm:
                    wi_slots[m + 1] = rowblock_into(wis, wi_d, m + 1, winat[(m + 1) % 2])
                    wf_slots[m + 1] = rowblock_into(wfs, wf_d, m + 1, wfnat[(m + 1) % 2])
                wait("tensor", ve, copy_evB[m])
                for h in range(2):
                    for which in ("i", "f"):
                        ps_ = pi if which == "i" else pf
                        wT_ = wiT if which == "i" else wfT
                        relv = pi_rel if which == "i" else pf_rel
                        if relv is not None:
                            if which == "i":
                                wait("tensor", ve, relv)
                            else:
                                wait("tensor", ac, relv)

                        def mmgrp(e, _m=m, _h=h, _ps=ps_, _wT=wT_):
                            ins = None
                            for k in range(TD):
                                lhsT = _wT[_m % 2][:, k * 128:(k + 1) * 128]
                                for n2 in range(2):
                                    nn = _h * 1024 + n2 * 512
                                    ins = e.matmul(out=_ps[:, n2 * 512:(n2 + 1) * 512],
                                                   lhsT=lhsT,
                                                   rhs=bufA[:, k * 2048 + nn:k * 2048 + nn + 512],
                                                   start=(k == 0), stop=(k == TD - 1))
                            return ins
                        vmm = emit("tensor", mmgrp, inc=te)
                        mm_lastB[m] = vmm
                        wait("scalar", te, vmm)
                        if u_ev is not None:
                            wait("scalar", ve, u_ev)
                        if which == "i":
                            a1 = emit("scalar", lambda e, _h=h: e.activation(
                                out=bX[:, _h * 1024:_h * 1024 + 1024], in_=pi[:],
                                func=AF.Sigmoid), inc=ac)
                            wait("vector", ac, a1)
                            pi_rel = emit("vector", lambda e, _h=h: e.tensor_tensor(
                                out=bX[:, _h * 1024:_h * 1024 + 1024],
                                in0=bX[:, _h * 1024:_h * 1024 + 1024],
                                in1=pi[:], op=ALU.mult), inc=ve)
                        else:
                            pf_rel = emit("scalar", lambda e, _h=h: e.activation(
                                out=bGate[:, _h * 1024:_h * 1024 + 1024], in_=pf[:],
                                func=AF.Sigmoid), inc=ac)
                            wait("vector", ac, pf_rel)
                            emit("vector", lambda e, _h=h: e.scalar_tensor_tensor(
                                out=bX[:, _h * 1024:_h * 1024 + 1024],
                                in0=bGate[:, _h * 1024:_h * 1024 + 1024], scalar=1.0,
                                in1=bX[:, _h * 1024:_h * 1024 + 1024],
                                op0=ALU.subtract, op1=ALU.mult))
                    if h == 0 and m + 1 < TEm:
                        if_prep(m + 1)

                v_scan = emit("vector", lambda e: e.tensor_tensor_scan(
                    out=bX, data0=bGate, data1=bX, initial=0.0,
                    op0=ALU.mult, op1=ALU.subtract), inc=ve)
                wait("scalar", ve, v_scan)
                a_sw = emit("scalar", lambda e: e.activation(
                    out=bGate, in_=bX, func=AF.Sigmoid), inc=ac)
                wait("vector", ac, a_sw)
                emit("vector", lambda e: e.tensor_tensor(out=bX, in0=bX, in1=bGate, op=ALU.mult))
                u_ev = emit("vector", lambda e, _m=m: e.tensor_tensor(
                    out=bufB[:, _m * 2048:(_m + 1) * 2048], in0=bX,
                    in1=bufB[:, _m * 2048:(_m + 1) * 2048], op=ALU.mult), inc=ve)
        TE_B = te.n
        VE_B_PSUM = pi_rel
        AC_B_PSUM = pf_rel
        xs.rel[0] = [(ve, ve.n)]
        xs.rel[1] = [(ve, ve.n)]

        wait("tensor", ve, VE_B_PSUM)
        wait("tensor", ac, AC_B_PSUM)
        with nc.psum_tensor("po0", [128, 1024], F32) as po0, \
             nc.psum_tensor("po1", [128, 1024], F32) as po1, \
             nc.psum_tensor("slWo", [128, 2048], BF16) as slWo:
            pos = [po0, po1]
            po_rel = [None, None]
            slWo_rel = None
            copy_evO = {}
            mm_lastO = {}

            def wo_prep(m):
                nonlocal slWo_rel
                slot = wo_slots[m]
                use("tensor", slot)
                if slWo_rel is not None:
                    wait("tensor", ve, slWo_rel)

                def wtr(e, _i=slot[0]):
                    ins = None
                    for k in range(TD):
                        ins = e.transpose(out=slWo[:, k * 128:(k + 1) * 128],
                                          in_=natsl[_i][:, k * 128:(k + 1) * 128],
                                          identity=ident[:])
                    return ins
                vte = emit("tensor", wtr, inc=te)
                wait("vector", te, vte)
                if m >= 2 and (m - 2) in mm_lastO:
                    wait("vector", te, mm_lastO[m - 2])
                else:
                    wait("vector", te, TE_B0)
                slWo_rel = emit("vector", lambda e, _m=m: e.tensor_copy(
                    out=tsl[_m % 2][:], in_=slWo[:]), inc=ve)
                copy_evO[m] = slWo_rel
                wgs.release(slot, [(te, vte)])

            wo_prep(0)
            for m in range(TD):
                if m + 1 < TD and (m + 1) not in wo_slots:
                    wo_slots[m + 1] = rowblock_into(wgs, wo_d, m + 1, natsl[(m + 1) % 2])
                wait("tensor", ve, copy_evO[m])
                for h in range(2):
                    if po_rel[h] is not None:
                        wait("tensor", ve, po_rel[h])

                    def ommgrp(e, _m=m, _h=h):
                        ins = None
                        for k in range(TEm):
                            lhsT = tsl[_m % 2][:, k * 128:(k + 1) * 128]
                            for n2 in range(2):
                                nn = _h * 1024 + n2 * 512
                                ins = e.matmul(out=pos[_h][:, n2 * 512:(n2 + 1) * 512],
                                               lhsT=lhsT,
                                               rhs=bufB[:, k * 2048 + nn:k * 2048 + nn + 512],
                                               start=(k == 0), stop=(k == TEm - 1))
                        return ins
                    vmm = emit("tensor", ommgrp, inc=te)
                    mm_lastO[m] = vmm
                    wait("vector", te, vmm)
                    po_rel[h] = emit("vector", lambda e, _m=m, _h=h: e.tensor_copy(
                        out=bufA[:, _m * 2048 + _h * 1024:_m * 2048 + _h * 1024 + 1024],
                        in_=pos[_h][:]), inc=ve)
                    if h == 0 and m + 1 < TD:
                        wo_prep(m + 1)
        VE_ATTN = ve.n
        TE_B3 = te.n

        wait("tensor", ve, ve.n)
        wait("tensor", ac, ac.n)
        with nc.psum_tensor("slT0", [128, 2048], BF16) as slT0, \
             nc.psum_tensor("slT1", [128, 2048], BF16) as slT1:
            slTs = [slT0, slT1]
            slT_rel = [None, None]
            abuf = [gvw[:, 2048:4096], gvw[:, 4096:6144]]
            wait("gpsimd", te, TE_B3)
            xq0.n += 16
            on("gpsimd", lambda e, _h=xq0.h: e.dma_start(
                out=bufB[:, 0:4 * 2048].rearrange("p (t d) -> p t d", t=4),
                in_=xin[0:4 * 128, :].rearrange("(t p) d -> p t d", p=128)).then_inc(_h, 16))
            xq1.n += 16
            on("gpsimd", lambda e, _h=xq1.h: e.dma_start(
                out=bufB[:, 4 * 2048:16 * 2048].rearrange("p (t d) -> p t d", t=12),
                in_=xin[4 * 128:16 * 128, :].rearrange("(t p) d -> p t d", p=128)).then_inc(_h, 16))
            for t in range(TL):
                sl = t % 2
                wait("tensor", ve, VE_ATTN)
                if slT_rel[sl] is not None:
                    wait("tensor", ve, slT_rel[sl])

                def tgrp(e, _t=t, _sl=sl):
                    ins = None
                    for j in range(TD):
                        ins = e.transpose(out=slTs[_sl][:, j * 128:(j + 1) * 128],
                                          in_=bufA[:, j * 2048 + _t * 128:j * 2048 + _t * 128 + 128],
                                          identity=ident[:])
                    return ins
                vte = emit("tensor", tgrp, inc=te)
                i = t % 2
                if xs.rel[i]:
                    for (rs, rv) in xs.rel[i]:
                        wait("vector", rs, rv)
                    xs.rel[i] = None
                wait("vector", te, vte)
                wait("vector", ve, v_rms)
                v_r = emit("vector", lambda e, _i=i, _t=t, _sl=sl: e.tensor_scalar_mul(
                    abuf[_i], slTs[_sl][:], st1(RMS, _t)), inc=ve)
                slT_rel[sl] = v_r
                wait("gpsimd", ve, v_r)
                wait("gpsimd", ypre, 16)
                spl = spill.load(lambda e, _i=i, _t=t: e.dma_start(
                    out=yout[_t * 128:(_t + 1) * 128, :], in_=abuf[_i],
                    accum_op=ALU.add))
                wait("vector", xq0 if t < 4 else xq1, 16)
                v_cp = emit("vector", lambda e, _i=i, _t=t: e.scalar_tensor_tensor(
                    out=bufB[:, _t * 2048:(_t + 1) * 2048], in0=abuf[_i], scalar=1.0,
                    in1=bufB[:, _t * 2048:(_t + 1) * 2048],
                    op0=ALU.mult, op1=ALU.add, accum_out=st1(SUM, _t)), inc=ve)
                wait("scalar", ve, v_cp)
                emit("scalar", lambda e, _t=t: e.activation(
                    out=gvw[:, 0:2048], in_=bufB[:, _t * 2048:(_t + 1) * 2048],
                    func=AF.Square, accum_out=st1(SQ, _t)), inc=ac)
                xs.rel[i] = [(ve, v_cp), (spl[1], spl[2])]

        v_inv2 = batch_invstd()

        wait("vector", ve, v_inv2)
        wait("vector", ac, ac.n)
        for t in range(TL):
            emit("vector", lambda e, _t=t: e.tensor_scalar(
                bufB[:, _t * 2048:(_t + 1) * 2048], bufB[:, _t * 2048:(_t + 1) * 2048],
                st1(INV, _t), st1(LNB, _t), ALU.mult, ALU.add), inc=ve)
        VE_APPLY2 = ve.n

        wait("tensor", ve, ve.n)
        wait("tensor", ac, ac.n)
        with nc.psum_tensor("slC0", [128, 2048], BF16) as slC0, \
             nc.psum_tensor("slC1", [128, 2048], BF16) as slC1:
            wait("tensor", ve, VE_APPLY2)
            transpose_tm_to_fm(bufB, bufA, [slC0, slC1])

        rwbf = gvw[0:8, 0:2048]
        for s_ in spill.sems:
            wait("vector", s_)
            wait("scalar", s_)
        wait("vector", ac, ac.n)
        wait("scalar", ve, ve.n)
        wait("gpsimd", te, te.n)
        m_rw = misc_load("gpsimd", lambda e: e.dma_start(out=rwbf, in_=rw_d))
        wait("tensor", ve, ve.n)
        wait("tensor", ac, ac.n)
        with nc.psum_tensor("psRT", [128, 128], BF16) as psRT, \
             nc.psum_tensor("psL", [128, 2048], F32) as psL, \
             nc.psum_tensor("psLT", [128, 128], F32) as psLT, \
             nc.psum_tensor("psD", [128, 2048], BF16) as psD:
            wait("tensor", miscg, m_rw)

            def rtr(e):
                ins = None
                for k in range(TD):
                    ins = e.transpose(out=psRT[:, k * 8:(k + 1) * 8],
                                      in_=rwbf[:, k * 128:(k + 1) * 128],
                                      identity=ident[0:8, 0:8])
                return ins
            vte = emit("tensor", rtr, inc=te)
            wait("vector", te, vte)
            v_rt = emit("vector", lambda e: e.tensor_copy(out=routerT[:], in_=psRT[:]), inc=ve)

            wait("tensor", ve, v_rt)

            def lmm(e):
                ins = None
                for n in range(4):
                    for k in range(TD):
                        ins = e.matmul(out=psL[0:8, n * 512:(n + 1) * 512],
                                       lhsT=routerT[:, k * 8:(k + 1) * 8],
                                       rhs=bufA[:, k * 2048 + n * 512:k * 2048 + n * 512 + 512],
                                       start=(k == 0), stop=(k == TD - 1))
                return ins
            v_lg = emit("tensor", lmm, inc=te)
            wait("vector", te, v_lg)
            logf = fsc[0:8, 0:2048]
            v_lf = emit("vector", lambda e: e.tensor_copy(out=logf, in_=psL[0:8, :]), inc=ve)
            wait("tensor", ve, v_lf)

            def ltr(e):
                ins = None
                for m in range(TL):
                    ins = e.transpose(out=psLT[:, m * 8:(m + 1) * 8],
                                      in_=logf[0:8, m * 128:(m + 1) * 128],
                                      identity=identf8[:])
                return ins
            v_lt = emit("tensor", ltr, inc=te)
            wait("vector", te, v_lt)
            emit("vector", lambda e: e.tensor_copy(out=logtm[:], in_=psLT[:]))

            for m in range(TL):
                emit("vector", lambda e, _m=m: e.max(
                    out=top8s[:, _m * 8:(_m + 1) * 8], in_=logtm[:, _m * 8:(_m + 1) * 8]))
            t8v = top8s[:].rearrange("p (m c) -> p m c", m=TL)
            v_da = emit("vector", lambda e: e.tensor_tensor(
                out=st_c(DALL), in0=t8v[:, :, 1:2], in1=t8v[:, :, 0:1],
                op=ALU.subtract), inc=ve)
            wait("scalar", ve, v_da)
            a_w2 = emit("scalar", lambda e: e.activation(
                out=st_c(WAcol), in_=st_c(DALL), func=AF.Sigmoid), inc=ac)
            wait("vector", ac, a_w2)
            emit("vector", lambda e: e.tensor_scalar(
                st_c(WBcol), st_c(WAcol), -1.0, 1.0, ALU.mult, ALU.add))
            for m in range(TL):
                lg_m = logtm[:, m * 8:(m + 1) * 8]
                emit("vector", lambda e, _l=lg_m, _m=m: e.tensor_scalar(
                    tmp8a[:], _l, top8s[:, _m * 8:_m * 8 + 1], st1(WBcol, _m),
                    ALU.is_equal, ALU.mult))
                emit("vector", lambda e, _l=lg_m, _m=m: e.tensor_scalar(
                    tmp8b[:], _l, top8s[:, _m * 8 + 1:_m * 8 + 2], st1(WAcol, _m),
                    ALU.is_equal, ALU.mult))
                emit("vector", lambda e, _m=m: e.tensor_add(
                    wdense[:, _m * 8:(_m + 1) * 8], tmp8a[:], tmp8b[:]))
            v_wd = emit("vector", lambda e: e.tensor_scalar_add(tmp8a[:], tmp8a[:], 0.0), inc=ve)

            wait("tensor", ve, v_wd)

            def wdtr(e):
                ins = None
                for m in range(TL):
                    ins = e.transpose(out=psD[0:8, m * 128:(m + 1) * 128],
                                      in_=wdense[:, m * 8:(m + 1) * 8],
                                      identity=ident[:])
                return ins
            v_wt = emit("tensor", wdtr, inc=te)
            wait("vector", te, v_wt)
            v_wfm = emit("vector", lambda e: e.tensor_copy(out=wfm[:], in_=psD[0:8, :]), inc=ve)

        c1sl = [wsc[:, 0:2048], wsc[:, 2048:4096]]
        c2sl = [wsc[:, 4096:6144], wsc[:, 6144:8192]]
        c3sl = [wsc[:, 8192:9216], wsc[:, 9216:10240]]
        vs_scr = scr2[:, 0:1024]

        wait("gpsimd", te, TE_B3)

        def col_load(stream, dram, e_, blk, dst, ktiles):
            return stream.load(lambda eng, _e=e_, _b=blk, _d=dst, _k=ktiles: eng.dma_start(
                out=_d.rearrange("p (k c) -> p k c", k=_k),
                in_=dram[_e, :, _b * 128:(_b + 1) * 128].rearrange("(k p) c -> p k c", p=128)))

        wait("tensor", ve, ve.n)
        wait("tensor", ac, ac.n)
        with nc.psum_tensor("pga", [128, 1024], F32) as pga, \
             nc.psum_tensor("pva", [128, 1024], F32) as pva, \
             nc.psum_tensor("pw0", [128, 1024], F32) as pw0, \
             nc.psum_tensor("pw1", [128, 1024], F32) as pw1:
            pws = [pw0, pw1]
            pga_rel = None
            pva_rel = None
            pw_rel = [None, None]
            last_gv = None

            c1q = {(0, 0): col_load(wc1, w1_d, 0, 0, c1sl[0], 16)}
            c2q = {(0, 0): col_load(wc2, w2_d, 0, 0, c2sl[0], 16)}
            c3q = {}

            for e_ in range(NEXP):
                wait("tensor", miscs, MISC_S)
                wait("tensor", ve, v_wfm)
                for hb in range(2):
                    if pw_rel[hb] is not None:
                        wait("tensor", ve, pw_rel[hb])

                    def bc(e, _e=e_, _hb=hb):
                        ins = None
                        for n2 in range(2):
                            ins = e.matmul(out=pws[_hb][:, n2 * 512:(n2 + 1) * 512],
                                           lhsT=selsb[:, _e * 128:(_e + 1) * 128],
                                           rhs=wfm[0:8,
                                                   _hb * 1024 + n2 * 512:_hb * 1024 + (n2 + 1) * 512],
                                           start=True, stop=True)
                        return ins
                    v_b = emit("tensor", bc, inc=te)
                    wait("vector", te, v_b)
                    pw_rel[hb] = emit("vector", lambda e, _hb=hb: e.tensor_copy(
                        out=wb_ap[:, _hb * 1024:_hb * 1024 + 1024], in_=pws[_hb][:]), inc=ve)
                v_wb = pw_rel[1]

                for fm in range(TFm):
                    nb = fm + 1
                    if nb < TFm:
                        c1q[(e_, nb)] = col_load(wc1, w1_d, e_, nb, c1sl[nb % 2], 16)
                        c2q[(e_, nb)] = col_load(wc2, w2_d, e_, nb, c2sl[nb % 2], 16)
                    elif e_ + 1 < NEXP:
                        c1q[(e_ + 1, 0)] = col_load(wc1, w1_d, e_ + 1, 0, c1sl[nb % 2], 16)
                        c2q[(e_ + 1, 0)] = col_load(wc2, w2_d, e_ + 1, 0, c2sl[nb % 2], 16)
                    s1 = c1q[(e_, fm)]
                    s2 = c2q[(e_, fm)]
                    use("tensor", s1)
                    use("tensor", s2)
                    v_g = v_v = None
                    for h in range(2):
                        if pga_rel is not None:
                            wait("tensor", ve, pga_rel)

                        def g1(e, _h=h, _i=s1[0]):
                            ins = None
                            for k in range(TD):
                                lhsT = c1sl[_i][:, k * 128:(k + 1) * 128]
                                for n2 in range(2):
                                    nn = _h * 1024 + n2 * 512
                                    ins = e.matmul(out=pga[:, n2 * 512:(n2 + 1) * 512],
                                                   lhsT=lhsT,
                                                   rhs=bufA[:, k * 2048 + nn:k * 2048 + nn + 512],
                                                   start=(k == 0), stop=(k == TD - 1))
                            return ins
                        v_g = emit("tensor", g1, inc=te)
                        if pva_rel is not None:
                            wait("tensor", ve, pva_rel)

                        def g2(e, _h=h, _i=s2[0]):
                            ins = None
                            for k in range(TD):
                                lhsT = c2sl[_i][:, k * 128:(k + 1) * 128]
                                for n2 in range(2):
                                    nn = _h * 1024 + n2 * 512
                                    ins = e.matmul(out=pva[:, n2 * 512:(n2 + 1) * 512],
                                                   lhsT=lhsT,
                                                   rhs=bufA[:, k * 2048 + nn:k * 2048 + nn + 512],
                                                   start=(k == 0), stop=(k == TD - 1))
                            return ins
                        v_v = emit("tensor", g2, inc=te)

                        wait("scalar", te, v_g)
                        a_sg = emit("scalar", lambda e: e.activation(
                            out=sg_ap, in_=pga[:], func=AF.Sigmoid), inc=ac)
                        wait("vector", ac, a_sg)
                        gv_sl = gvw[:, fm * 2048 + h * 1024:fm * 2048 + h * 1024 + 1024]
                        pga_rel = emit("vector", lambda e, _g=gv_sl: e.tensor_tensor(
                            out=_g, in0=sg_ap, in1=pga[:], op=ALU.mult), inc=ve)
                        wait("vector", te, v_v)
                        wait("vector", ve, v_wb)
                        pva_rel = emit("vector", lambda e, _h=h: e.tensor_tensor(
                            out=vs_scr, in0=pva[:], in1=wb_ap[:, _h * 1024:_h * 1024 + 1024],
                            op=ALU.mult), inc=ve)
                        last_gv = emit("vector", lambda e, _g=gv_sl: e.tensor_tensor(
                            out=_g, in0=_g, in1=vs_scr, op=ALU.mult), inc=ve)
                    wc1.release(s1, [(te, v_g)])
                    wc2.release(s2, [(te, v_v)])

                for mD in range(TD):
                    if (e_, mD) not in c3q:
                        c3q[(e_, mD)] = col_load(wc3, w3_d, e_, mD, c3sl[mD % 2], KW3)
                    if mD + 1 < TD:
                        c3q[(e_, mD + 1)] = col_load(wc3, w3_d, e_, mD + 1, c3sl[(mD + 1) % 2], KW3)
                    elif e_ + 1 < NEXP:
                        c3q[(e_ + 1, 0)] = col_load(wc3, w3_d, e_ + 1, 0, c3sl[(mD + 1) % 2], KW3)
                    s3 = c3q[(e_, mD)]
                    use("tensor", s3)
                    wait("tensor", ve, last_gv)
                    v_w3 = None
                    for h in range(2):
                        sl = h
                        if pw_rel[sl] is not None:
                            wait("tensor", ve, pw_rel[sl])

                        def w3g(e, _mD=mD, _h=h, _i=s3[0], _sl=sl):
                            ins = None
                            for k in range(KW3):
                                lhsT = c3sl[_i][:, k * 128:(k + 1) * 128]
                                for n2 in range(2):
                                    nn = _h * 1024 + n2 * 512
                                    ins = e.matmul(out=pws[_sl][:, n2 * 512:(n2 + 1) * 512],
                                                   lhsT=lhsT,
                                                   rhs=gvw[:, k * 2048 + nn:k * 2048 + nn + 512],
                                                   start=(k == 0), stop=(k == KW3 - 1))
                            return ins
                        v_w3 = emit("tensor", w3g, inc=te)
                        wait("vector", te, v_w3)
                        dst = bufB[:, mD * 2048 + h * 1024:mD * 2048 + h * 1024 + 1024]
                        if e_ == 0:
                            pw_rel[sl] = emit("vector", lambda e, _d=dst, _sl=sl: e.tensor_copy(
                                out=_d, in_=pws[_sl][:]), inc=ve)
                        else:
                            pw_rel[sl] = emit("vector", lambda e, _d=dst, _sl=sl: e.tensor_tensor(
                                out=_d, in0=pws[_sl][:], in1=_d, op=ALU.add), inc=ve)
                    wc3.release(s3, [(te, v_w3)])
        VE_FFN = ve.n
        xs.rel[0] = [(ve, VE_FFN)]
        xs.rel[1] = [(ve, VE_FFN)]

        wait("tensor", ve, ve.n)
        wait("tensor", ac, ac.n)
        with nc.psum_tensor("slE0", [128, 2048], BF16) as slE0, \
             nc.psum_tensor("slE1", [128, 2048], BF16) as slE1:
            slEs = [slE0, slE1]
            slE_rel = [None, None]
            for s_ in spill.sems:
                wait("gpsimd", s_)
            ebuf = [gvw[:, k * 2048:(k + 1) * 2048] for k in range(4)]
            ebuf_rel = [None] * 4
            for t in range(TL):
                sl = t % 2
                wait("tensor", ve, VE_FFN)
                if slE_rel[sl] is not None:
                    wait("tensor", ve, slE_rel[sl])

                def tgrp(e, _t=t, _sl=sl):
                    ins = None
                    for j in range(TD):
                        ins = e.transpose(out=slEs[_sl][:, j * 128:(j + 1) * 128],
                                          in_=bufB[:, j * 2048 + _t * 128:j * 2048 + _t * 128 + 128],
                                          identity=ident[:])
                    return ins
                vte = emit("tensor", tgrp, inc=te)
                i = t % 4
                if ebuf_rel[i]:
                    for (rs, rv) in ebuf_rel[i]:
                        wait("vector", rs, rv)
                    ebuf_rel[i] = None
                wait("vector", te, vte)
                v_o = emit("vector", lambda e, _i=i, _sl=sl: e.tensor_copy(
                    out=ebuf[_i], in_=slEs[_sl][:]), inc=ve)
                slE_rel[sl] = v_o
                wait("gpsimd", ve, v_o)
                so = accs.load(lambda e, _i=i, _t=t: e.dma_start(
                    out=yout[_t * 128:(_t + 1) * 128, :], in_=ebuf[_i],
                    accum_op=ALU.add))
                ebuf_rel[i] = [(so[1], so[2])]

        for s_ in accs.sems:
            wait("gpsimd", s_)

        block = ctx.enter_context(nc.Block())

        @block.sync
        def _(eng):
            for f in plan["sync"]:
                f(eng)

        @block.gpsimd
        def _(eng):
            for f in plan["gpsimd"]:
                f(eng)

        @block.tensor
        def _(eng):
            for f in plan["tensor"]:
                f(eng)

        @block.vector
        def _(eng):
            for f in plan["vector"]:
                f(eng)

        @block.scalar
        def _(eng):
            for f in plan["scalar"]:
                f(eng)

    return nc


def kernel(L0_attn_hidden_states, L0_attn_Wi, L0_attn_Wf, L0_attn_Wg,
           L0_attn_g_norm_weight, L0_attn_Wo, L0_ffn_router_w,
           L0_ffn_w1, L0_ffn_w2, L0_ffn_w3):
    x = np.ascontiguousarray(np.asarray(L0_attn_hidden_states, dtype=np.float32))
    ws = {
        "wi": np.ascontiguousarray(np.asarray(L0_attn_Wi, dtype=np.float32)),
        "wf": np.ascontiguousarray(np.asarray(L0_attn_Wf, dtype=np.float32)),
        "wg": np.ascontiguousarray(np.asarray(L0_attn_Wg, dtype=np.float32)),
        "gwv": np.ascontiguousarray(np.asarray(L0_attn_g_norm_weight, dtype=np.float32)),
        "wo": np.ascontiguousarray(np.asarray(L0_attn_Wo, dtype=np.float32)),
        "rw": np.ascontiguousarray(np.asarray(L0_ffn_router_w, dtype=np.float32)),
        "w1": np.ascontiguousarray(np.asarray(L0_ffn_w1, dtype=np.float32)),
        "w2": np.ascontiguousarray(np.asarray(L0_ffn_w2, dtype=np.float32)),
        "w3": np.ascontiguousarray(np.asarray(L0_ffn_w3, dtype=np.float32)),
    }
    if "nc" not in _CACHED:
        _CACHED["nc"] = build()
    nc = _CACHED["nc"]
    in_maps = [dict(ws, x=x[i]) for i in range(8)]
    res = run_bass_kernel_spmd(nc, in_maps, core_ids=list(range(8)))
    out = np.stack([np.asarray(res.results[i]["y"], dtype=np.float32) for i in range(8)], axis=0)
    return out


# version 13
# speedup vs baseline: 1.0551x; 1.0551x over previous
import numpy as np
import ml_dtypes
from contextlib import ExitStack
import concourse.bass as bass
import concourse.mybir as mybir
from concourse.bass_utils import run_bass_kernel_spmd

F32 = mybir.dt.float32
BF16 = mybir.dt.bfloat16
AF = mybir.ActivationFunctionType
ALU = mybir.AluOpType
AX = mybir.AxisListType

L = 2048
DM = 2048
EM = 2048
FF = 1024
NEXP = 8
TL = 16
TD = 16
TEm = 16
TFm = 8
KW3 = 8
EPS = 1e-5
ENGS = ("sync", "gpsimd", "tensor", "vector", "scalar")

_CACHED = {}


def build():
    nc = bass.Bass("TRN2", target_bir_lowering=False)

    xin = nc.dram_tensor("x", [L, DM], F32, kind="ExternalInput").ap()
    wi_d = nc.dram_tensor("wi", [EM, DM], F32, kind="ExternalInput").ap()
    wf_d = nc.dram_tensor("wf", [EM, DM], F32, kind="ExternalInput").ap()
    wg_d = nc.dram_tensor("wg", [EM, DM], F32, kind="ExternalInput").ap()
    gw_d = nc.dram_tensor("gwv", [EM], F32, kind="ExternalInput").ap()
    wo_d = nc.dram_tensor("wo", [DM, EM], F32, kind="ExternalInput").ap()
    rw_d = nc.dram_tensor("rw", [NEXP, DM], F32, kind="ExternalInput").ap()
    w1_d = nc.dram_tensor("w1", [NEXP, DM, FF], F32, kind="ExternalInput").ap()
    w2_d = nc.dram_tensor("w2", [NEXP, DM, FF], F32, kind="ExternalInput").ap()
    w3_d = nc.dram_tensor("w3", [NEXP, FF, DM], F32, kind="ExternalInput").ap()
    yout = nc.dram_tensor("y", [L, DM], F32, kind="ExternalOutput").ap()
    xbf_d = nc.dram_tensor("xbf", [L, DM], BF16, kind="Internal").ap()

    ident_dram = nc.inline_tensor(np.eye(128, dtype=ml_dtypes.bfloat16), name="identc")
    identf8_dram = nc.inline_tensor(np.eye(8, dtype=np.float32), name="identf8c")
    sel_np = np.zeros((NEXP, NEXP * 128), dtype=ml_dtypes.bfloat16)
    for _e in range(NEXP):
        sel_np[_e, _e * 128:(_e + 1) * 128] = 1.0
    sel_dram = nc.inline_tensor(sel_np, name="selc")

    plan = {e: [] for e in ENGS}
    waited = {e: {} for e in ENGS}

    def on(eng, f):
        plan[eng].append(f)

    with ExitStack() as ctx:
        class S:
            __slots__ = ("h", "n", "name")

            def __init__(self, name):
                self.h = ctx.enter_context(nc.semaphore(name))
                self.n = 0
                self.name = name

        te = S("te")
        ve = S("ve")
        ac = S("ac")
        miscs = S("miscs")
        miscg = S("miscg")

        def wait(eng, s, val=None):
            v = s.n if val is None else val
            if v is None or v <= 0:
                return
            if waited[eng].get(s.name, 0) >= v:
                return
            waited[eng][s.name] = v
            on(eng, lambda e, _h=s.h, _v=v: e.wait_ge(_h, _v))

        def emit(eng, f, inc=None, n=1):
            if eng == "scalar":
                wait(eng, ac, ac.n)
                if inc is None:
                    inc = ac
            elif eng == "vector":
                wait(eng, ve, ve.n)
                if inc is None:
                    inc = ve
            if inc is None:
                on(eng, f)
                return None
            inc.n += n
            on(eng, lambda e, _f=f, _h=inc.h, _n=n: _f(e).then_inc(_h, _n))
            return inc.n

        class Stream:

            def __init__(self, name, nslots, queue):
                self.q = queue
                self.sems = [S(f"{name}{i}") for i in range(nslots)]
                self.rel = [None] * nslots
                self.k = 0

            def load(self, fdma):
                i = self.k % len(self.sems)
                s = self.sems[i]
                if self.rel[i]:
                    for (rs, rv) in self.rel[i]:
                        wait(self.q, rs, rv)
                    self.rel[i] = None
                s.n += 16
                on(self.q, lambda e, _f=fdma, _h=s.h: _f(e).then_inc(_h, 16))
                self.k += 1
                return (i, s, s.n)

            def release(self, slot, events):
                self.rel[slot[0]] = events

        def use(eng, slot):
            wait(eng, slot[1], slot[2])

        sb = lambda n_, s_, d_: ctx.enter_context(nc.sbuf_tensor(n_, s_, d_))
        bufA = sb("bufA", [128, TL * L], BF16)
        bufB = sb("bufB", [128, TL * L], BF16)
        gvw = sb("gvw", [128, 8 * L], BF16)
        wsc = sb("wsc", [128, 10 * 1024], BF16)
        fsc = sb("fsc", [128, 2 * 2048], F32)
        scr2 = sb("scr2", [128, 1024], BF16)
        ident = sb("identsb", [128, 128], BF16)
        identf8 = sb("identf8sb", [8, 8], F32)
        gwt = sb("gwt", [128, 16], F32)
        stats = sb("stats", [128, 256], F32)
        routerT = sb("routerTsb", [128, 128], BF16)
        logtm = sb("logtm", [128, 128], F32)
        top8s = sb("top8s", [128, 128], F32)
        wdense = sb("wdense", [128, 128], BF16)
        wfm = sb("wfm", [8, L], BF16)
        tmp8a = sb("tmp8a", [128, 8], F32)
        tmp8b = sb("tmp8b", [128, 8], F32)
        onesbf = sb("onesbf", [1, 128], BF16)
        selsb = sb("selsb", [8, NEXP * 128], BF16)

        rem = nc.sbuf_bytes_remaining
        assert rem >= 0, f"SBUF over budget by {-rem} bytes"

        fs = [fsc[:, i * 2048:(i + 1) * 2048] for i in range(2)]
        xs_ap = [fs[0], fs[1]]
        bGate, bX = fs[0], fs[1]
        wb_ap = fs[0]
        sg_ap = fsc[:, 2048:2048 + 1024]
        dummy_bf = bufB[:, 0:2048]

        SUM, SQ, MEAN, VAR, INV, LNB = 0, 16, 32, 48, 64, 80
        SS, RMS, T0, T1 = 96, 112, 128, 144
        DALL, WAcol, WBcol = 160, 176, 192

        st_c = lambda c0: stats[:, c0:c0 + 16]
        st1 = lambda c0, t: stats[:, c0 + t:c0 + t + 1]

        xs = Stream("xs", 2, "sync")
        ypre = S("ypre")
        xq0 = S("xq0")
        xq1 = S("xq1")
        spill = Stream("sp", 2, "gpsimd")
        accs = Stream("oa", 4, "gpsimd")
        wgs = Stream("wgs", 2, "gpsimd")
        wis = Stream("wis", 2, "gpsimd")
        wfs = Stream("wfs", 2, "gpsimd")
        wc1 = Stream("wc1", 2, "gpsimd")
        wc2 = Stream("wc2", 2, "gpsimd")
        wc3 = Stream("wc3", 2, "gpsimd")

        def misc_load(queue, fdma):
            ms = miscs if queue == "sync" else miscg
            ms.n += 16
            on(queue, lambda e, _f=fdma, _h=ms.h: _f(e).then_inc(_h, 16))
            return ms.n

        misc_load("sync", lambda e: e.dma_start(out=ident[:], in_=ident_dram.ap()))
        misc_load("sync", lambda e: e.dma_start(out=identf8[:], in_=identf8_dram.ap()))
        misc_load("gpsimd", lambda e: e.dma_start(
            out=gwt[:], in_=gw_d.rearrange("(m p) -> p m", p=128),
            allow_slow_non_contiguous=True))
        misc_load("sync", lambda e: e.dma_start(out=selsb[:], in_=sel_dram.ap()))
        MISC_S = miscs.n
        MISC_G = miscg.n

        ev_ones = emit("vector", lambda e: e.memset(onesbf[:], 1.0), inc=ve)

        for t in range(TL):
            slot = xs.load(lambda e, _t=t, _i=xs.k % 2: e.dma_start(
                out=xs_ap[_i], in_=xin[_t * 128:(_t + 1) * 128, :]))
            use("scalar", slot)
            i = slot[0]
            emit("scalar", lambda e, _i=i, _t=t: e.activation(
                out=bufB[:, _t * 2048:(_t + 1) * 2048], in_=xs_ap[_i], func=AF.Copy,
                accum_out=st1(SUM, _t)), inc=ac)
            use("vector", slot)
            v = emit("vector", lambda e, _i=i, _t=t: e.scalar_tensor_tensor(
                out=gvw[:, 0:2048], in0=xs_ap[_i], scalar=1.0, in1=xs_ap[_i],
                op0=ALU.mult, op1=ALU.mult, accum_out=st1(SQ, _t)), inc=ve)
            xs.release(slot, [(ve, v), (ac, ac.n)])
        AC_STAGE1 = ac.n
        VE_STAGE1 = ve.n
        ypre.n += 16
        on("sync", lambda e, _h=ypre.h: e.dma_start(out=yout, in_=xin).then_inc(_h, 16))

        def batch_invstd():
            wait("vector", ac)
            emit("vector", lambda e: e.tensor_scalar_mul(st_c(MEAN), st_c(SUM), 1.0 / DM))
            emit("vector", lambda e: e.tensor_scalar_mul(st_c(VAR), st_c(SQ), 1.0 / DM))
            emit("vector", lambda e: e.tensor_mul(st_c(T0), st_c(MEAN), st_c(MEAN)))
            emit("vector", lambda e: e.tensor_sub(st_c(VAR), st_c(VAR), st_c(T0)))
            v0 = emit("vector", lambda e: e.tensor_scalar_add(st_c(VAR), st_c(VAR), EPS), inc=ve)
            wait("scalar", ve, v0)
            emit("scalar", lambda e: e.activation(out=st_c(T0), in_=st_c(VAR), func=AF.Ln))
            a0 = emit("scalar", lambda e: e.activation(out=st_c(INV), in_=st_c(T0), func=AF.Exp,
                                                       scale=-0.5), inc=ac)
            wait("vector", ac, a0)
            emit("vector", lambda e: e.tensor_mul(st_c(T1), st_c(INV), st_c(INV)))
            emit("vector", lambda e: e.tensor_mul(st_c(T1), st_c(T1), st_c(VAR)))
            emit("vector", lambda e: e.tensor_scalar(st_c(T1), st_c(T1), -0.5, 1.5, ALU.mult, ALU.add))
            emit("vector", lambda e: e.tensor_mul(st_c(INV), st_c(INV), st_c(T1)))
            emit("vector", lambda e: e.tensor_mul(st_c(LNB), st_c(MEAN), st_c(INV)))
            return emit("vector", lambda e: e.tensor_scalar_mul(st_c(LNB), st_c(LNB), -1.0), inc=ve)

        v_inv1 = batch_invstd()

        wait("vector", ve, v_inv1)
        wait("vector", ac, AC_STAGE1)
        wait("vector", ve, VE_STAGE1)
        for t in range(TL):
            emit("vector", lambda e, _t=t: e.tensor_scalar(
                bufB[:, _t * 2048:(_t + 1) * 2048], bufB[:, _t * 2048:(_t + 1) * 2048],
                st1(INV, _t), st1(LNB, _t), ALU.mult, ALU.add), inc=ve)
        VE_APPLY1 = ve.n

        def transpose_tm_to_fm(src_buf, dst_buf, slabs):
            slab_rel = [None, None]
            for j in range(TD):
                sl = j % 2
                if slab_rel[sl] is not None:
                    wait("tensor", ve, slab_rel[sl])

                def tgrp(e, _j=j, _sl=sl):
                    ins = None
                    for t_ in range(TL):
                        ins = e.transpose(out=slabs[_sl][:, t_ * 128:(t_ + 1) * 128],
                                          in_=src_buf[:, t_ * 2048 + _j * 128:
                                                      t_ * 2048 + _j * 128 + 128],
                                          identity=ident[:])
                    return ins
                vte = emit("tensor", tgrp, inc=te)
                wait("vector", te, vte)
                slab_rel[sl] = emit("vector", lambda e, _j=j, _sl=sl: e.tensor_copy(
                    out=dst_buf[:, _j * 2048:(_j + 1) * 2048], in_=slabs[_sl][:]), inc=ve)

        wait("tensor", ve, ve.n)
        wait("tensor", ac, ac.n)
        with nc.psum_tensor("slA0", [128, 2048], BF16) as slA0, \
             nc.psum_tensor("slA1", [128, 2048], BF16) as slA1:
            wait("tensor", miscs, MISC_S)
            wait("tensor", ve, VE_APPLY1)
            transpose_tm_to_fm(bufB, bufA, [slA0, slA1])
        TE_A = te.n
        VE_A = ve.n

        natsl = [wsc[:, 0:2048], wsc[:, 2048:4096]]
        tsl = [wsc[:, 4096:6144], wsc[:, 6144:8192]]
        gsqb = [gvw[:, 0:2048], gvw[:, 2048:4096]]

        def rowblock_into(stream, dram, m, dst):
            return stream.load(lambda e, _m=m, _d=dst: e.dma_start(
                out=_d, in_=dram[_m * 128:(_m + 1) * 128, :]))

        wait("tensor", ve, ve.n)
        wait("tensor", ac, ac.n)
        xq0.n += 16
        on("gpsimd", lambda e, _h=xq0.h: e.dma_start(
            out=xbf_d[0:4 * 128, :], in_=xin[0:4 * 128, :]).then_inc(_h, 16))
        xq1.n += 16
        on("gpsimd", lambda e, _h=xq1.h: e.dma_start(
            out=xbf_d[4 * 128:16 * 128, :], in_=xin[4 * 128:16 * 128, :]).then_inc(_h, 16))
        with nc.psum_tensor("pg0", [128, 1024], F32) as pg0, \
             nc.psum_tensor("pg1", [128, 1024], F32) as pg1, \
             nc.psum_tensor("slW", [128, 2048], BF16) as slW, \
             nc.psum_tensor("slQ", [128, 2048], BF16) as slQ:
            pgs = [pg0, pg1]
            pg_rel = [None, None]
            slW_rel = None
            slQ_rel = None
            sq_ev = {}

            wg_slots = {0: rowblock_into(wgs, wg_d, 0, natsl[0])}
            copy_ev = {}
            mm_last = {}

            def wg_prep(m):
                slot = wg_slots[m]
                use("tensor", slot)
                nonlocal slW_rel
                if slW_rel is not None:
                    wait("tensor", ve, slW_rel)

                def wtr(e, _i=slot[0]):
                    ins = None
                    for k in range(TD):
                        ins = e.transpose(out=slW[:, k * 128:(k + 1) * 128],
                                          in_=natsl[_i][:, k * 128:(k + 1) * 128],
                                          identity=ident[:])
                    return ins
                vte = emit("tensor", wtr, inc=te)
                wait("vector", te, vte)
                if m >= 2 and (m - 2) in mm_last:
                    wait("vector", te, mm_last[m - 2])
                slW_rel = emit("vector", lambda e, _m=m: e.tensor_copy(
                    out=tsl[_m % 2][:], in_=slW[:]), inc=ve)
                copy_ev[m] = slW_rel
                wgs.release(slot, [(te, vte)])

            wg_prep(0)
            for m in range(TEm):
                if m + 1 < TEm:
                    wg_slots[m + 1] = rowblock_into(wgs, wg_d, m + 1, natsl[(m + 1) % 2])
                wait("tensor", ve, copy_ev[m])
                for h in range(2):
                    if pg_rel[h] is not None:
                        wait("tensor", ac, pg_rel[h])

                    def gmm(e, _m=m, _h=h):
                        ins = None
                        for k in range(TD):
                            lhsT = tsl[_m % 2][:, k * 128:(k + 1) * 128]
                            for n2 in range(2):
                                nn = _h * 1024 + n2 * 512
                                ins = e.matmul(out=pgs[_h][:, n2 * 512:(n2 + 1) * 512],
                                               lhsT=lhsT,
                                               rhs=bufA[:, k * 2048 + nn:k * 2048 + nn + 512],
                                               start=(k == 0), stop=(k == TD - 1))
                        return ins
                    vmm = emit("tensor", gmm, inc=te)
                    mm_last[m] = vmm
                    wait("scalar", te, vmm)
                    wait("scalar", miscg, MISC_G)
                    emit("scalar", lambda e, _m=m, _h=h: e.activation(
                        out=bufB[:, _m * 2048 + _h * 1024:_m * 2048 + _h * 1024 + 1024],
                        in_=pgs[_h][:], func=AF.Copy, scale=gwt[:, _m:_m + 1]), inc=ac)
                    pg_rel[h] = emit("scalar", lambda e, _m=m, _h=h: e.activation(
                        out=gsqb[_m % 2][:, _h * 1024:_h * 1024 + 1024],
                        in_=pgs[_h][:], func=AF.Square), inc=ac)
                    if h == 0 and m + 1 < TEm:
                        wg_prep(m + 1)

                for mq in ([m - 1] if m >= 1 else []) + ([m] if m == TEm - 1 else []):
                    wait("tensor", ac, pg_rel[1] if mq == m else None) if mq == m else None
                    if mq == m:
                        wait("tensor", ac, pg_rel[1])
                    else:
                        wait("tensor", ac, sq_ev[mq])
                    if slQ_rel is not None:
                        wait("tensor", ve, slQ_rel)

                    def qtr(e, _m=mq):
                        ins = None
                        for t_ in range(TL):
                            ins = e.transpose(out=slQ[:, t_ * 128:(t_ + 1) * 128],
                                              in_=gsqb[_m % 2][:, t_ * 128:(t_ + 1) * 128],
                                              identity=ident[:])
                        return ins
                    vq = emit("tensor", qtr, inc=te)
                    wait("vector", te, vq)
                    if mq == 0:
                        slQ_rel = emit("vector", lambda e: e.tensor_reduce(
                            out=st_c(SS), in_=slQ[:].rearrange("p (t c) -> p t c", t=TL),
                            axis=AX.X, op=ALU.add), inc=ve)
                    else:
                        emit("vector", lambda e: e.tensor_reduce(
                            out=st_c(T0), in_=slQ[:].rearrange("p (t c) -> p t c", t=TL),
                            axis=AX.X, op=ALU.add))
                        slQ_rel = emit("vector", lambda e: e.tensor_add(
                            st_c(SS), st_c(SS), st_c(T0)), inc=ve)
                sq_ev[m] = pg_rel[1]

            v0 = emit("vector", lambda e: e.tensor_scalar(
                st_c(VAR), st_c(SS), 1.0 / EM, EPS, ALU.mult, ALU.add), inc=ve)
            wait("scalar", ve, v0)
            emit("scalar", lambda e: e.activation(out=st_c(T0), in_=st_c(VAR), func=AF.Ln))
            a0 = emit("scalar", lambda e: e.activation(out=st_c(RMS), in_=st_c(T0), func=AF.Exp,
                                                       scale=-0.5), inc=ac)
            wait("vector", ac, a0)
            emit("vector", lambda e: e.tensor_mul(st_c(T1), st_c(RMS), st_c(RMS)))
            emit("vector", lambda e: e.tensor_mul(st_c(T1), st_c(T1), st_c(VAR)))
            emit("vector", lambda e: e.tensor_scalar(st_c(T1), st_c(T1), -0.5, 1.5, ALU.mult, ALU.add))
            v_rms = emit("vector", lambda e: e.tensor_mul(st_c(RMS), st_c(RMS), st_c(T1)), inc=ve)
        TE_B0 = te.n

        winat = [gvw[:, 0:2048], gvw[:, 2048:4096]]
        wfnat = [gvw[:, 4096:6144], gvw[:, 6144:8192]]
        wiT = [gvw[:, 8192:10240], gvw[:, 10240:12288]]
        wfT = [gvw[:, 12288:14336], gvw[:, 14336:16384]]

        wait("gpsimd", te, TE_B0)
        wo_slots = {0: rowblock_into(wgs, wo_d, 0, natsl[0]),
                    1: rowblock_into(wgs, wo_d, 1, natsl[1])}

        wait("tensor", ve, ve.n)
        wait("tensor", ac, ac.n)
        with nc.psum_tensor("pi", [128, 1024], F32) as pi, \
             nc.psum_tensor("pf", [128, 1024], F32) as pf, \
             nc.psum_tensor("slWi", [128, 2048], BF16) as slWi, \
             nc.psum_tensor("slWf", [128, 2048], BF16) as slWf:
            slWi_rel = None
            slWf_rel = None
            pi_rel = None
            pf_rel = None
            wi_slots = {0: rowblock_into(wis, wi_d, 0, winat[0])}
            wf_slots = {0: rowblock_into(wfs, wf_d, 0, wfnat[0])}
            copy_evB = {}
            mm_lastB = {}
            u_ev = None

            def if_prep(m):
                nonlocal slWi_rel, slWf_rel
                for which in ("i", "f"):
                    slot = (wi_slots if which == "i" else wf_slots)[m]
                    nat = winat if which == "i" else wfnat
                    slab = slWi if which == "i" else slWf
                    tdst = wiT if which == "i" else wfT
                    rel = slWi_rel if which == "i" else slWf_rel
                    use("tensor", slot)
                    if rel is not None:
                        wait("tensor", ve, rel)

                    def wtr(e, _i=slot[0], _nat=nat, _slab=slab):
                        ins = None
                        for k in range(TD):
                            ins = e.transpose(out=_slab[:, k * 128:(k + 1) * 128],
                                              in_=_nat[_i][:, k * 128:(k + 1) * 128],
                                              identity=ident[:])
                        return ins
                    vte = emit("tensor", wtr, inc=te)
                    wait("vector", te, vte)
                    if m >= 2 and (m - 2) in mm_lastB:
                        wait("vector", te, mm_lastB[m - 2])
                    nrel = emit("vector", lambda e, _m=m, _slab=slab, _tdst=tdst: e.tensor_copy(
                        out=_tdst[_m % 2][:], in_=_slab[:]), inc=ve)
                    if which == "i":
                        slWi_rel = nrel
                    else:
                        slWf_rel = nrel
                    (wis if which == "i" else wfs).release(slot, [(te, vte)])
                copy_evB[m] = nrel

            if_prep(0)
            for m in range(TEm):
                if m + 1 < TEm:
                    wi_slots[m + 1] = rowblock_into(wis, wi_d, m + 1, winat[(m + 1) % 2])
                    wf_slots[m + 1] = rowblock_into(wfs, wf_d, m + 1, wfnat[(m + 1) % 2])
                wait("tensor", ve, copy_evB[m])
                for h in range(2):
                    for which in ("i", "f"):
                        ps_ = pi if which == "i" else pf
                        wT_ = wiT if which == "i" else wfT
                        relv = pi_rel if which == "i" else pf_rel
                        if relv is not None:
                            if which == "i":
                                wait("tensor", ve, relv)
                            else:
                                wait("tensor", ac, relv)

                        def mmgrp(e, _m=m, _h=h, _ps=ps_, _wT=wT_):
                            ins = None
                            for k in range(TD):
                                lhsT = _wT[_m % 2][:, k * 128:(k + 1) * 128]
                                for n2 in range(2):
                                    nn = _h * 1024 + n2 * 512
                                    ins = e.matmul(out=_ps[:, n2 * 512:(n2 + 1) * 512],
                                                   lhsT=lhsT,
                                                   rhs=bufA[:, k * 2048 + nn:k * 2048 + nn + 512],
                                                   start=(k == 0), stop=(k == TD - 1))
                            return ins
                        vmm = emit("tensor", mmgrp, inc=te)
                        mm_lastB[m] = vmm
                        wait("scalar", te, vmm)
                        if u_ev is not None:
                            wait("scalar", ve, u_ev)
                        if which == "i":
                            a1 = emit("scalar", lambda e, _h=h: e.activation(
                                out=bX[:, _h * 1024:_h * 1024 + 1024], in_=pi[:],
                                func=AF.Sigmoid), inc=ac)
                            wait("vector", ac, a1)
                            pi_rel = emit("vector", lambda e, _h=h: e.tensor_tensor(
                                out=bX[:, _h * 1024:_h * 1024 + 1024],
                                in0=bX[:, _h * 1024:_h * 1024 + 1024],
                                in1=pi[:], op=ALU.mult), inc=ve)
                        else:
                            pf_rel = emit("scalar", lambda e, _h=h: e.activation(
                                out=bGate[:, _h * 1024:_h * 1024 + 1024], in_=pf[:],
                                func=AF.Sigmoid), inc=ac)
                            wait("vector", ac, pf_rel)
                            emit("vector", lambda e, _h=h: e.scalar_tensor_tensor(
                                out=bX[:, _h * 1024:_h * 1024 + 1024],
                                in0=bGate[:, _h * 1024:_h * 1024 + 1024], scalar=1.0,
                                in1=bX[:, _h * 1024:_h * 1024 + 1024],
                                op0=ALU.subtract, op1=ALU.mult))
                    if h == 0 and m + 1 < TEm:
                        if_prep(m + 1)

                v_scan = emit("vector", lambda e: e.tensor_tensor_scan(
                    out=bX, data0=bGate, data1=bX, initial=0.0,
                    op0=ALU.mult, op1=ALU.subtract), inc=ve)
                wait("scalar", ve, v_scan)
                a_sw = emit("scalar", lambda e: e.activation(
                    out=bGate, in_=bX, func=AF.Sigmoid), inc=ac)
                wait("vector", ac, a_sw)
                emit("vector", lambda e: e.tensor_tensor(out=bX, in0=bX, in1=bGate, op=ALU.mult))
                u_ev = emit("vector", lambda e, _m=m: e.tensor_tensor(
                    out=bufB[:, _m * 2048:(_m + 1) * 2048], in0=bX,
                    in1=bufB[:, _m * 2048:(_m + 1) * 2048], op=ALU.mult), inc=ve)
        TE_B = te.n
        VE_B_PSUM = pi_rel
        AC_B_PSUM = pf_rel
        xs.rel[0] = [(ve, ve.n)]
        xs.rel[1] = [(ve, ve.n)]

        wait("tensor", ve, VE_B_PSUM)
        wait("tensor", ac, AC_B_PSUM)
        with nc.psum_tensor("po0", [128, 1024], F32) as po0, \
             nc.psum_tensor("po1", [128, 1024], F32) as po1, \
             nc.psum_tensor("slWo", [128, 2048], BF16) as slWo:
            pos = [po0, po1]
            po_rel = [None, None]
            slWo_rel = None
            copy_evO = {}
            mm_lastO = {}

            def wo_prep(m):
                nonlocal slWo_rel
                slot = wo_slots[m]
                use("tensor", slot)
                if slWo_rel is not None:
                    wait("tensor", ve, slWo_rel)

                def wtr(e, _i=slot[0]):
                    ins = None
                    for k in range(TD):
                        ins = e.transpose(out=slWo[:, k * 128:(k + 1) * 128],
                                          in_=natsl[_i][:, k * 128:(k + 1) * 128],
                                          identity=ident[:])
                    return ins
                vte = emit("tensor", wtr, inc=te)
                wait("vector", te, vte)
                if m >= 2 and (m - 2) in mm_lastO:
                    wait("vector", te, mm_lastO[m - 2])
                else:
                    wait("vector", te, TE_B0)
                slWo_rel = emit("vector", lambda e, _m=m: e.tensor_copy(
                    out=tsl[_m % 2][:], in_=slWo[:]), inc=ve)
                copy_evO[m] = slWo_rel
                wgs.release(slot, [(te, vte)])

            wo_prep(0)
            for m in range(TD):
                if m + 1 < TD and (m + 1) not in wo_slots:
                    wo_slots[m + 1] = rowblock_into(wgs, wo_d, m + 1, natsl[(m + 1) % 2])
                wait("tensor", ve, copy_evO[m])
                for h in range(2):
                    if po_rel[h] is not None:
                        wait("tensor", ve, po_rel[h])

                    def ommgrp(e, _m=m, _h=h):
                        ins = None
                        for k in range(TEm):
                            lhsT = tsl[_m % 2][:, k * 128:(k + 1) * 128]
                            for n2 in range(2):
                                nn = _h * 1024 + n2 * 512
                                ins = e.matmul(out=pos[_h][:, n2 * 512:(n2 + 1) * 512],
                                               lhsT=lhsT,
                                               rhs=bufB[:, k * 2048 + nn:k * 2048 + nn + 512],
                                               start=(k == 0), stop=(k == TEm - 1))
                        return ins
                    vmm = emit("tensor", ommgrp, inc=te)
                    mm_lastO[m] = vmm
                    wait("vector", te, vmm)
                    po_rel[h] = emit("vector", lambda e, _m=m, _h=h: e.tensor_copy(
                        out=bufA[:, _m * 2048 + _h * 1024:_m * 2048 + _h * 1024 + 1024],
                        in_=pos[_h][:]), inc=ve)
                    if h == 0 and m + 1 < TD:
                        wo_prep(m + 1)
        VE_ATTN = ve.n
        TE_B3 = te.n

        wait("tensor", ve, ve.n)
        wait("tensor", ac, ac.n)
        with nc.psum_tensor("slT0", [128, 2048], BF16) as slT0, \
             nc.psum_tensor("slT1", [128, 2048], BF16) as slT1:
            slTs = [slT0, slT1]
            slT_rel = [None, None]
            abuf = [gvw[:, 2048:4096], gvw[:, 4096:6144]]
            wait("gpsimd", te, TE_B3)
            wait("gpsimd", xq0, 16)
            wait("gpsimd", xq1, 16)
            xch = [S(f"xch{k}") for k in range(4)]
            for k in range(4):
                xch[k].n += 16
                on("gpsimd", lambda e, _k=k, _h=xch[k].h: e.dma_start(
                    out=bufB[:, _k * 4 * 2048:(_k + 1) * 4 * 2048].rearrange(
                        "p (t d) -> p t d", t=4),
                    in_=xbf_d[_k * 4 * 128:(_k + 1) * 4 * 128, :].rearrange(
                        "(t p) d -> p t d", p=128)).then_inc(_h, 16))
            for t in range(TL):
                sl = t % 2
                wait("tensor", ve, VE_ATTN)
                if slT_rel[sl] is not None:
                    wait("tensor", ve, slT_rel[sl])

                def tgrp(e, _t=t, _sl=sl):
                    ins = None
                    for j in range(TD):
                        ins = e.transpose(out=slTs[_sl][:, j * 128:(j + 1) * 128],
                                          in_=bufA[:, j * 2048 + _t * 128:j * 2048 + _t * 128 + 128],
                                          identity=ident[:])
                    return ins
                vte = emit("tensor", tgrp, inc=te)
                i = t % 2
                if xs.rel[i]:
                    for (rs, rv) in xs.rel[i]:
                        wait("vector", rs, rv)
                    xs.rel[i] = None
                wait("vector", te, vte)
                wait("vector", ve, v_rms)
                v_r = emit("vector", lambda e, _i=i, _t=t, _sl=sl: e.tensor_scalar_mul(
                    abuf[_i], slTs[_sl][:], st1(RMS, _t)), inc=ve)
                slT_rel[sl] = v_r
                wait("gpsimd", ve, v_r)
                wait("gpsimd", ypre, 16)
                spl = spill.load(lambda e, _i=i, _t=t: e.dma_start(
                    out=yout[_t * 128:(_t + 1) * 128, :], in_=abuf[_i],
                    accum_op=ALU.add))
                wait("vector", xch[t // 4], 16)
                v_cp = emit("vector", lambda e, _i=i, _t=t: e.scalar_tensor_tensor(
                    out=bufB[:, _t * 2048:(_t + 1) * 2048], in0=abuf[_i], scalar=1.0,
                    in1=bufB[:, _t * 2048:(_t + 1) * 2048],
                    op0=ALU.mult, op1=ALU.add, accum_out=st1(SUM, _t)), inc=ve)
                wait("scalar", ve, v_cp)
                emit("scalar", lambda e, _t=t: e.activation(
                    out=gvw[:, 0:2048], in_=bufB[:, _t * 2048:(_t + 1) * 2048],
                    func=AF.Square, accum_out=st1(SQ, _t)), inc=ac)
                xs.rel[i] = [(ve, v_cp), (spl[1], spl[2])]

        v_inv2 = batch_invstd()

        wait("vector", ve, v_inv2)
        wait("vector", ac, ac.n)
        for t in range(TL):
            emit("vector", lambda e, _t=t: e.tensor_scalar(
                bufB[:, _t * 2048:(_t + 1) * 2048], bufB[:, _t * 2048:(_t + 1) * 2048],
                st1(INV, _t), st1(LNB, _t), ALU.mult, ALU.add), inc=ve)
        VE_APPLY2 = ve.n

        wait("tensor", ve, ve.n)
        wait("tensor", ac, ac.n)
        with nc.psum_tensor("slC0", [128, 2048], BF16) as slC0, \
             nc.psum_tensor("slC1", [128, 2048], BF16) as slC1:
            wait("tensor", ve, VE_APPLY2)
            transpose_tm_to_fm(bufB, bufA, [slC0, slC1])

        rwbf = gvw[0:8, 0:2048]
        for s_ in spill.sems:
            wait("vector", s_)
            wait("scalar", s_)
        wait("vector", ac, ac.n)
        wait("scalar", ve, ve.n)
        wait("gpsimd", te, te.n)
        m_rw = misc_load("gpsimd", lambda e: e.dma_start(out=rwbf, in_=rw_d))
        wait("tensor", ve, ve.n)
        wait("tensor", ac, ac.n)
        with nc.psum_tensor("psRT", [128, 128], BF16) as psRT, \
             nc.psum_tensor("psL", [128, 2048], F32) as psL, \
             nc.psum_tensor("psLT", [128, 128], F32) as psLT, \
             nc.psum_tensor("psD", [128, 2048], BF16) as psD:
            wait("tensor", miscg, m_rw)

            def rtr(e):
                ins = None
                for k in range(TD):
                    ins = e.transpose(out=psRT[:, k * 8:(k + 1) * 8],
                                      in_=rwbf[:, k * 128:(k + 1) * 128],
                                      identity=ident[0:8, 0:8])
                return ins
            vte = emit("tensor", rtr, inc=te)
            wait("vector", te, vte)
            v_rt = emit("vector", lambda e: e.tensor_copy(out=routerT[:], in_=psRT[:]), inc=ve)

            wait("tensor", ve, v_rt)

            def lmm(e):
                ins = None
                for n in range(4):
                    for k in range(TD):
                        ins = e.matmul(out=psL[0:8, n * 512:(n + 1) * 512],
                                       lhsT=routerT[:, k * 8:(k + 1) * 8],
                                       rhs=bufA[:, k * 2048 + n * 512:k * 2048 + n * 512 + 512],
                                       start=(k == 0), stop=(k == TD - 1))
                return ins
            v_lg = emit("tensor", lmm, inc=te)
            wait("vector", te, v_lg)
            logf = fsc[0:8, 0:2048]
            v_lf = emit("vector", lambda e: e.tensor_copy(out=logf, in_=psL[0:8, :]), inc=ve)
            wait("tensor", ve, v_lf)

            def ltr(e):
                ins = None
                for m in range(TL):
                    ins = e.transpose(out=psLT[:, m * 8:(m + 1) * 8],
                                      in_=logf[0:8, m * 128:(m + 1) * 128],
                                      identity=identf8[:])
                return ins
            v_lt = emit("tensor", ltr, inc=te)
            wait("vector", te, v_lt)
            emit("vector", lambda e: e.tensor_copy(out=logtm[:], in_=psLT[:]))

            for m in range(TL):
                emit("vector", lambda e, _m=m: e.max(
                    out=top8s[:, _m * 8:(_m + 1) * 8], in_=logtm[:, _m * 8:(_m + 1) * 8]))
            t8v = top8s[:].rearrange("p (m c) -> p m c", m=TL)
            v_da = emit("vector", lambda e: e.tensor_tensor(
                out=st_c(DALL), in0=t8v[:, :, 1:2], in1=t8v[:, :, 0:1],
                op=ALU.subtract), inc=ve)
            wait("scalar", ve, v_da)
            a_w2 = emit("scalar", lambda e: e.activation(
                out=st_c(WAcol), in_=st_c(DALL), func=AF.Sigmoid), inc=ac)
            wait("vector", ac, a_w2)
            emit("vector", lambda e: e.tensor_scalar(
                st_c(WBcol), st_c(WAcol), -1.0, 1.0, ALU.mult, ALU.add))
            for m in range(TL):
                lg_m = logtm[:, m * 8:(m + 1) * 8]
                emit("vector", lambda e, _l=lg_m, _m=m: e.tensor_scalar(
                    tmp8a[:], _l, top8s[:, _m * 8:_m * 8 + 1], st1(WBcol, _m),
                    ALU.is_equal, ALU.mult))
                emit("vector", lambda e, _l=lg_m, _m=m: e.tensor_scalar(
                    tmp8b[:], _l, top8s[:, _m * 8 + 1:_m * 8 + 2], st1(WAcol, _m),
                    ALU.is_equal, ALU.mult))
                emit("vector", lambda e, _m=m: e.tensor_add(
                    wdense[:, _m * 8:(_m + 1) * 8], tmp8a[:], tmp8b[:]))
            v_wd = emit("vector", lambda e: e.tensor_scalar_add(tmp8a[:], tmp8a[:], 0.0), inc=ve)

            wait("tensor", ve, v_wd)

            def wdtr(e):
                ins = None
                for m in range(TL):
                    ins = e.transpose(out=psD[0:8, m * 128:(m + 1) * 128],
                                      in_=wdense[:, m * 8:(m + 1) * 8],
                                      identity=ident[:])
                return ins
            v_wt = emit("tensor", wdtr, inc=te)
            wait("vector", te, v_wt)
            v_wfm = emit("vector", lambda e: e.tensor_copy(out=wfm[:], in_=psD[0:8, :]), inc=ve)

        c1sl = [wsc[:, 0:2048], wsc[:, 2048:4096]]
        c2sl = [wsc[:, 4096:6144], wsc[:, 6144:8192]]
        c3sl = [wsc[:, 8192:9216], wsc[:, 9216:10240]]
        vs_scr = scr2[:, 0:1024]

        wait("gpsimd", te, TE_B3)

        def col_load(stream, dram, e_, blk, dst, ktiles):
            return stream.load(lambda eng, _e=e_, _b=blk, _d=dst, _k=ktiles: eng.dma_start(
                out=_d.rearrange("p (k c) -> p k c", k=_k),
                in_=dram[_e, :, _b * 128:(_b + 1) * 128].rearrange("(k p) c -> p k c", p=128)))

        wait("tensor", ve, ve.n)
        wait("tensor", ac, ac.n)
        with nc.psum_tensor("pga", [128, 1024], F32) as pga, \
             nc.psum_tensor("pva", [128, 1024], F32) as pva, \
             nc.psum_tensor("pw0", [128, 1024], F32) as pw0, \
             nc.psum_tensor("pw1", [128, 1024], F32) as pw1:
            pws = [pw0, pw1]
            pga_rel = None
            pva_rel = None
            pw_rel = [None, None]
            last_gv = None
            gv_ev7 = {}

            c1q = {(0, 0): col_load(wc1, w1_d, 0, 0, c1sl[0], 16)}
            c2q = {(0, 0): col_load(wc2, w2_d, 0, 0, c2sl[0], 16)}
            c3q = {}

            for e_ in range(NEXP):
                wait("tensor", miscs, MISC_S)
                wait("tensor", ve, v_wfm)
                for hb in range(2):
                    if pw_rel[hb] is not None:
                        wait("tensor", ve, pw_rel[hb])

                    def bc(e, _e=e_, _hb=hb):
                        ins = None
                        for n2 in range(2):
                            ins = e.matmul(out=pws[_hb][:, n2 * 512:(n2 + 1) * 512],
                                           lhsT=selsb[:, _e * 128:(_e + 1) * 128],
                                           rhs=wfm[0:8,
                                                   _hb * 1024 + n2 * 512:_hb * 1024 + (n2 + 1) * 512],
                                           start=True, stop=True)
                        return ins
                    v_b = emit("tensor", bc, inc=te)
                    wait("vector", te, v_b)
                    pw_rel[hb] = emit("vector", lambda e, _hb=hb: e.tensor_copy(
                        out=wb_ap[:, _hb * 1024:_hb * 1024 + 1024], in_=pws[_hb][:]), inc=ve)
                v_wb = pw_rel[1]

                for fm in range(TFm):
                    nb = fm + 1
                    if nb < TFm:
                        c1q[(e_, nb)] = col_load(wc1, w1_d, e_, nb, c1sl[nb % 2], 16)
                        c2q[(e_, nb)] = col_load(wc2, w2_d, e_, nb, c2sl[nb % 2], 16)
                    elif e_ + 1 < NEXP:
                        c1q[(e_ + 1, 0)] = col_load(wc1, w1_d, e_ + 1, 0, c1sl[nb % 2], 16)
                        c2q[(e_ + 1, 0)] = col_load(wc2, w2_d, e_ + 1, 0, c2sl[nb % 2], 16)
                    s1 = c1q[(e_, fm)]
                    s2 = c2q[(e_, fm)]
                    use("tensor", s1)
                    use("tensor", s2)
                    v_g = v_v = None
                    for h in range(2):
                        if pga_rel is not None:
                            wait("tensor", ve, pga_rel)

                        def g1(e, _h=h, _i=s1[0]):
                            ins = None
                            for k in range(TD):
                                lhsT = c1sl[_i][:, k * 128:(k + 1) * 128]
                                for n2 in range(2):
                                    nn = _h * 1024 + n2 * 512
                                    ins = e.matmul(out=pga[:, n2 * 512:(n2 + 1) * 512],
                                                   lhsT=lhsT,
                                                   rhs=bufA[:, k * 2048 + nn:k * 2048 + nn + 512],
                                                   start=(k == 0), stop=(k == TD - 1))
                            return ins
                        v_g = emit("tensor", g1, inc=te)
                        if pva_rel is not None:
                            wait("tensor", ve, pva_rel)

                        def g2(e, _h=h, _i=s2[0]):
                            ins = None
                            for k in range(TD):
                                lhsT = c2sl[_i][:, k * 128:(k + 1) * 128]
                                for n2 in range(2):
                                    nn = _h * 1024 + n2 * 512
                                    ins = e.matmul(out=pva[:, n2 * 512:(n2 + 1) * 512],
                                                   lhsT=lhsT,
                                                   rhs=bufA[:, k * 2048 + nn:k * 2048 + nn + 512],
                                                   start=(k == 0), stop=(k == TD - 1))
                            return ins
                        v_v = emit("tensor", g2, inc=te)

                        wait("scalar", te, v_g)
                        a_sg = emit("scalar", lambda e: e.activation(
                            out=sg_ap, in_=pga[:], func=AF.Sigmoid), inc=ac)
                        wait("vector", ac, a_sg)
                        gv_sl = gvw[:, fm * 2048 + h * 1024:fm * 2048 + h * 1024 + 1024]
                        pga_rel = emit("vector", lambda e, _g=gv_sl: e.tensor_tensor(
                            out=_g, in0=sg_ap, in1=pga[:], op=ALU.mult), inc=ve)
                        wait("vector", te, v_v)
                        wait("vector", ve, v_wb)
                        pva_rel = emit("vector", lambda e, _h=h: e.tensor_tensor(
                            out=vs_scr, in0=pva[:], in1=wb_ap[:, _h * 1024:_h * 1024 + 1024],
                            op=ALU.mult), inc=ve)
                        last_gv = emit("vector", lambda e, _g=gv_sl: e.tensor_tensor(
                            out=_g, in0=_g, in1=vs_scr, op=ALU.mult), inc=ve)
                        gv_ev7[fm] = last_gv
                    wc1.release(s1, [(te, v_g)])
                    wc2.release(s2, [(te, v_v)])

                for mD in range(TD):
                    if (e_, mD) not in c3q:
                        c3q[(e_, mD)] = col_load(wc3, w3_d, e_, mD, c3sl[mD % 2], KW3)
                    if mD + 1 < TD:
                        c3q[(e_, mD + 1)] = col_load(wc3, w3_d, e_, mD + 1, c3sl[(mD + 1) % 2], KW3)
                    elif e_ + 1 < NEXP:
                        c3q[(e_ + 1, 0)] = col_load(wc3, w3_d, e_ + 1, 0, c3sl[(mD + 1) % 2], KW3)
                    s3 = c3q[(e_, mD)]
                    use("tensor", s3)
                    if mD > 0:
                        wait("tensor", ve, last_gv)
                    v_w3 = None
                    for h in range(2):
                        sl = h
                        if pw_rel[sl] is not None:
                            wait("tensor", ve, pw_rel[sl])

                        if mD == 0:
                            wait("tensor", ve, gv_ev7[6])

                            def w3a(e, _h=h, _i=s3[0], _sl=sl):
                                ins = None
                                for k in range(KW3 - 1):
                                    lhsT = c3sl[_i][:, k * 128:(k + 1) * 128]
                                    for n2 in range(2):
                                        nn = _h * 1024 + n2 * 512
                                        ins = e.matmul(out=pws[_sl][:, n2 * 512:(n2 + 1) * 512],
                                                       lhsT=lhsT,
                                                       rhs=gvw[:, k * 2048 + nn:k * 2048 + nn + 512],
                                                       start=(k == 0), stop=False)
                                return ins
                            emit("tensor", w3a, inc=te)
                            wait("tensor", ve, last_gv)

                            def w3b(e, _h=h, _i=s3[0], _sl=sl):
                                ins = None
                                k = KW3 - 1
                                lhsT = c3sl[_i][:, k * 128:(k + 1) * 128]
                                for n2 in range(2):
                                    nn = _h * 1024 + n2 * 512
                                    ins = e.matmul(out=pws[_sl][:, n2 * 512:(n2 + 1) * 512],
                                                   lhsT=lhsT,
                                                   rhs=gvw[:, k * 2048 + nn:k * 2048 + nn + 512],
                                                   start=False, stop=True)
                                return ins
                            v_w3 = emit("tensor", w3b, inc=te)
                            continue_body = True
                        else:
                            def w3g(e, _mD=mD, _h=h, _i=s3[0], _sl=sl):
                                ins = None
                                for k in range(KW3):
                                    lhsT = c3sl[_i][:, k * 128:(k + 1) * 128]
                                    for n2 in range(2):
                                        nn = _h * 1024 + n2 * 512
                                        ins = e.matmul(out=pws[_sl][:, n2 * 512:(n2 + 1) * 512],
                                                       lhsT=lhsT,
                                                       rhs=gvw[:, k * 2048 + nn:k * 2048 + nn + 512],
                                                       start=(k == 0), stop=(k == KW3 - 1))
                                return ins
                            v_w3 = emit("tensor", w3g, inc=te)
                        wait("vector", te, v_w3)
                        dst = bufB[:, mD * 2048 + h * 1024:mD * 2048 + h * 1024 + 1024]
                        if e_ == 0:
                            pw_rel[sl] = emit("vector", lambda e, _d=dst, _sl=sl: e.tensor_copy(
                                out=_d, in_=pws[_sl][:]), inc=ve)
                        else:
                            pw_rel[sl] = emit("vector", lambda e, _d=dst, _sl=sl: e.tensor_tensor(
                                out=_d, in0=pws[_sl][:], in1=_d, op=ALU.add), inc=ve)
                    wc3.release(s3, [(te, v_w3)])
        VE_FFN = ve.n
        xs.rel[0] = [(ve, VE_FFN)]
        xs.rel[1] = [(ve, VE_FFN)]

        wait("tensor", ve, ve.n)
        wait("tensor", ac, ac.n)
        with nc.psum_tensor("slE0", [128, 2048], BF16) as slE0, \
             nc.psum_tensor("slE1", [128, 2048], BF16) as slE1:
            slEs = [slE0, slE1]
            slE_rel = [None, None]
            for s_ in spill.sems:
                wait("gpsimd", s_)
            ebuf = [gvw[:, k * 2048:(k + 1) * 2048] for k in range(4)]
            ebuf_rel = [None] * 4
            for t in range(TL):
                sl = t % 2
                wait("tensor", ve, VE_FFN)
                if slE_rel[sl] is not None:
                    wait("tensor", ve, slE_rel[sl])

                def tgrp(e, _t=t, _sl=sl):
                    ins = None
                    for j in range(TD):
                        ins = e.transpose(out=slEs[_sl][:, j * 128:(j + 1) * 128],
                                          in_=bufB[:, j * 2048 + _t * 128:j * 2048 + _t * 128 + 128],
                                          identity=ident[:])
                    return ins
                vte = emit("tensor", tgrp, inc=te)
                i = t % 4
                if ebuf_rel[i]:
                    for (rs, rv) in ebuf_rel[i]:
                        wait("vector", rs, rv)
                    ebuf_rel[i] = None
                wait("vector", te, vte)
                v_o = emit("vector", lambda e, _i=i, _sl=sl: e.tensor_copy(
                    out=ebuf[_i], in_=slEs[_sl][:]), inc=ve)
                slE_rel[sl] = v_o
                wait("gpsimd", ve, v_o)
                so = accs.load(lambda e, _i=i, _t=t: e.dma_start(
                    out=yout[_t * 128:(_t + 1) * 128, :], in_=ebuf[_i],
                    accum_op=ALU.add))
                ebuf_rel[i] = [(so[1], so[2])]

        for s_ in accs.sems:
            wait("gpsimd", s_)

        block = ctx.enter_context(nc.Block())

        @block.sync
        def _(eng):
            for f in plan["sync"]:
                f(eng)

        @block.gpsimd
        def _(eng):
            for f in plan["gpsimd"]:
                f(eng)

        @block.tensor
        def _(eng):
            for f in plan["tensor"]:
                f(eng)

        @block.vector
        def _(eng):
            for f in plan["vector"]:
                f(eng)

        @block.scalar
        def _(eng):
            for f in plan["scalar"]:
                f(eng)

    return nc


def kernel(L0_attn_hidden_states, L0_attn_Wi, L0_attn_Wf, L0_attn_Wg,
           L0_attn_g_norm_weight, L0_attn_Wo, L0_ffn_router_w,
           L0_ffn_w1, L0_ffn_w2, L0_ffn_w3):
    x = np.ascontiguousarray(np.asarray(L0_attn_hidden_states, dtype=np.float32))
    ws = {
        "wi": np.ascontiguousarray(np.asarray(L0_attn_Wi, dtype=np.float32)),
        "wf": np.ascontiguousarray(np.asarray(L0_attn_Wf, dtype=np.float32)),
        "wg": np.ascontiguousarray(np.asarray(L0_attn_Wg, dtype=np.float32)),
        "gwv": np.ascontiguousarray(np.asarray(L0_attn_g_norm_weight, dtype=np.float32)),
        "wo": np.ascontiguousarray(np.asarray(L0_attn_Wo, dtype=np.float32)),
        "rw": np.ascontiguousarray(np.asarray(L0_ffn_router_w, dtype=np.float32)),
        "w1": np.ascontiguousarray(np.asarray(L0_ffn_w1, dtype=np.float32)),
        "w2": np.ascontiguousarray(np.asarray(L0_ffn_w2, dtype=np.float32)),
        "w3": np.ascontiguousarray(np.asarray(L0_ffn_w3, dtype=np.float32)),
    }
    if "nc" not in _CACHED:
        _CACHED["nc"] = build()
    nc = _CACHED["nc"]
    in_maps = [dict(ws, x=x[i]) for i in range(8)]
    res = run_bass_kernel_spmd(nc, in_maps, core_ids=list(range(8)))
    out = np.stack([np.asarray(res.results[i]["y"], dtype=np.float32) for i in range(8)], axis=0)
    return out


# version 14
# speedup vs baseline: 1.1995x; 1.1369x over previous
import numpy as np
import ml_dtypes
from contextlib import ExitStack
import concourse.bass as bass
import concourse.mybir as mybir
from concourse.bass_utils import run_bass_kernel_spmd

F32 = mybir.dt.float32
BF16 = mybir.dt.bfloat16
AF = mybir.ActivationFunctionType
ALU = mybir.AluOpType
AX = mybir.AxisListType

L = 2048
DM = 2048
EM = 2048
FF = 1024
NEXP = 8
TL = 16
TD = 16
TEm = 16
TFm = 8
KW3 = 8
EPS = 1e-5
ENGS = ("sync", "gpsimd", "tensor", "vector", "scalar")

_CACHED = {}


def build():
    nc = bass.Bass("TRN2", target_bir_lowering=False)

    xin = nc.dram_tensor("x", [L, DM], F32, kind="ExternalInput").ap()
    wi_d = nc.dram_tensor("wi", [EM, DM], F32, kind="ExternalInput").ap()
    wf_d = nc.dram_tensor("wf", [EM, DM], F32, kind="ExternalInput").ap()
    wg_d = nc.dram_tensor("wg", [EM, DM], F32, kind="ExternalInput").ap()
    gw_d = nc.dram_tensor("gwv", [EM], F32, kind="ExternalInput").ap()
    wo_d = nc.dram_tensor("wo", [DM, EM], F32, kind="ExternalInput").ap()
    rw_d = nc.dram_tensor("rw", [NEXP, DM], F32, kind="ExternalInput").ap()
    w1_d = nc.dram_tensor("w1", [NEXP, DM, FF], F32, kind="ExternalInput").ap()
    w2_d = nc.dram_tensor("w2", [NEXP, DM, FF], F32, kind="ExternalInput").ap()
    w3_d = nc.dram_tensor("w3", [NEXP, FF, DM], F32, kind="ExternalInput").ap()
    yout = nc.dram_tensor("y", [L, DM], F32, kind="ExternalOutput").ap()
    xbf_d = nc.dram_tensor("xbf", [L, DM], BF16, kind="Internal").ap()

    ident_dram = nc.inline_tensor(np.eye(128, dtype=ml_dtypes.bfloat16), name="identc")
    identf8_dram = nc.inline_tensor(np.eye(8, dtype=np.float32), name="identf8c")
    sel_np = np.zeros((NEXP, NEXP * 128), dtype=ml_dtypes.bfloat16)
    for _e in range(NEXP):
        sel_np[_e, _e * 128:(_e + 1) * 128] = 1.0
    sel_dram = nc.inline_tensor(sel_np, name="selc")

    plan = {e: [] for e in ENGS}
    waited = {e: {} for e in ENGS}

    def on(eng, f):
        plan[eng].append(f)

    with ExitStack() as ctx:
        class S:
            __slots__ = ("h", "n", "name")

            def __init__(self, name):
                self.h = ctx.enter_context(nc.semaphore(name))
                self.n = 0
                self.name = name

        te = S("te")
        ve = S("ve")
        ac = S("ac")
        miscs = S("miscs")
        miscg = S("miscg")

        def wait(eng, s, val=None):
            v = s.n if val is None else val
            if v is None or v <= 0:
                return
            if waited[eng].get(s.name, 0) >= v:
                return
            waited[eng][s.name] = v
            on(eng, lambda e, _h=s.h, _v=v: e.wait_ge(_h, _v))

        def emit(eng, f, inc=None, n=1):
            if eng == "scalar":
                wait(eng, ac, ac.n)
                if inc is None:
                    inc = ac
            elif eng == "vector":
                wait(eng, ve, ve.n)
                if inc is None:
                    inc = ve
            if inc is None:
                on(eng, f)
                return None
            inc.n += n
            on(eng, lambda e, _f=f, _h=inc.h, _n=n: _f(e).then_inc(_h, _n))
            return inc.n

        class Stream:

            def __init__(self, name, nslots, queue):
                self.q = queue
                self.sems = [S(f"{name}{i}") for i in range(nslots)]
                self.rel = [None] * nslots
                self.k = 0

            def load(self, fdma):
                i = self.k % len(self.sems)
                s = self.sems[i]
                if self.rel[i]:
                    for (rs, rv) in self.rel[i]:
                        wait(self.q, rs, rv)
                    self.rel[i] = None
                s.n += 16
                on(self.q, lambda e, _f=fdma, _h=s.h: _f(e).then_inc(_h, 16))
                self.k += 1
                return (i, s, s.n)

            def release(self, slot, events):
                self.rel[slot[0]] = events

        def use(eng, slot):
            wait(eng, slot[1], slot[2])

        sb = lambda n_, s_, d_: ctx.enter_context(nc.sbuf_tensor(n_, s_, d_))
        bufA = sb("bufA", [128, TL * L], BF16)
        bufB = sb("bufB", [128, TL * L], BF16)
        gvw = sb("gvw", [128, 8 * L], BF16)
        wsc = sb("wsc", [128, 10 * 1024], BF16)
        fsc = sb("fsc", [128, 2 * 2048], F32)
        scr2 = sb("scr2", [128, 1024], BF16)
        ident = sb("identsb", [128, 128], BF16)
        identf8 = sb("identf8sb", [8, 8], F32)
        gwt = sb("gwt", [128, 16], F32)
        stats = sb("stats", [128, 256], F32)
        routerT = sb("routerTsb", [128, 128], BF16)
        logtm = sb("logtm", [128, 128], F32)
        top8s = sb("top8s", [128, 128], F32)
        wdense = sb("wdense", [128, 128], BF16)
        wfm = sb("wfm", [8, L], BF16)
        tmp8a = sb("tmp8a", [128, 8], F32)
        tmp8b = sb("tmp8b", [128, 8], F32)
        onesbf = sb("onesbf", [1, 128], BF16)
        selsb = sb("selsb", [8, NEXP * 128], BF16)

        rem = nc.sbuf_bytes_remaining
        assert rem >= 0, f"SBUF over budget by {-rem} bytes"

        fs = [fsc[:, i * 2048:(i + 1) * 2048] for i in range(2)]
        xs_ap = [fs[0], fs[1]]
        bGate, bX = fs[0], fs[1]
        wb_ap = fs[0]
        sg_ap = fsc[:, 2048:2048 + 1024]
        dummy_bf = bufB[:, 0:2048]

        SUM, SQ, MEAN, VAR, INV, LNB = 0, 16, 32, 48, 64, 80
        SS, RMS, T0, T1 = 96, 112, 128, 144
        DALL, WAcol, WBcol = 160, 176, 192

        st_c = lambda c0: stats[:, c0:c0 + 16]
        st1 = lambda c0, t: stats[:, c0 + t:c0 + t + 1]

        xs = Stream("xs", 2, "sync")
        ypre = S("ypre")
        xq0 = S("xq0")
        xq1 = S("xq1")
        spill = Stream("sp", 2, "gpsimd")
        accs = Stream("oa", 4, "gpsimd")
        wgs = Stream("wgs", 2, "gpsimd")
        wis = Stream("wis", 2, "gpsimd")
        wfs = Stream("wfs", 2, "gpsimd")
        wc1 = Stream("wc1", 2, "gpsimd")
        wc2 = Stream("wc2", 2, "gpsimd")
        wc3 = Stream("wc3", 2, "gpsimd")

        def misc_load(queue, fdma):
            ms = miscs if queue == "sync" else miscg
            ms.n += 16
            on(queue, lambda e, _f=fdma, _h=ms.h: _f(e).then_inc(_h, 16))
            return ms.n

        misc_load("sync", lambda e: e.dma_start(out=ident[:], in_=ident_dram.ap()))
        misc_load("sync", lambda e: e.dma_start(out=identf8[:], in_=identf8_dram.ap()))
        misc_load("gpsimd", lambda e: e.dma_start(
            out=gwt[:], in_=gw_d.rearrange("(m p) -> p m", p=128),
            allow_slow_non_contiguous=True))
        misc_load("sync", lambda e: e.dma_start(out=selsb[:], in_=sel_dram.ap()))
        MISC_S = miscs.n
        MISC_G = miscg.n

        ev_ones = emit("vector", lambda e: e.memset(onesbf[:], 1.0), inc=ve)

        for t in range(TL):
            slot = xs.load(lambda e, _t=t, _i=xs.k % 2: e.dma_start(
                out=xs_ap[_i], in_=xin[_t * 128:(_t + 1) * 128, :]))
            use("scalar", slot)
            i = slot[0]
            emit("scalar", lambda e, _i=i, _t=t: e.activation(
                out=bufB[:, _t * 2048:(_t + 1) * 2048], in_=xs_ap[_i], func=AF.Copy,
                accum_out=st1(SUM, _t)), inc=ac)
            use("vector", slot)
            v = emit("vector", lambda e, _i=i, _t=t: e.scalar_tensor_tensor(
                out=gvw[:, 0:2048], in0=xs_ap[_i], scalar=1.0, in1=xs_ap[_i],
                op0=ALU.mult, op1=ALU.mult, accum_out=st1(SQ, _t)), inc=ve)
            xs.release(slot, [(ve, v), (ac, ac.n)])
        AC_STAGE1 = ac.n
        VE_STAGE1 = ve.n
        ypre.n += 16
        on("sync", lambda e, _h=ypre.h: e.dma_start(out=yout, in_=xin).then_inc(_h, 16))

        def batch_invstd():
            wait("vector", ac)
            emit("vector", lambda e: e.tensor_scalar_mul(st_c(MEAN), st_c(SUM), 1.0 / DM))
            emit("vector", lambda e: e.tensor_scalar_mul(st_c(VAR), st_c(SQ), 1.0 / DM))
            emit("vector", lambda e: e.tensor_mul(st_c(T0), st_c(MEAN), st_c(MEAN)))
            emit("vector", lambda e: e.tensor_sub(st_c(VAR), st_c(VAR), st_c(T0)))
            v0 = emit("vector", lambda e: e.tensor_scalar_add(st_c(VAR), st_c(VAR), EPS), inc=ve)
            wait("scalar", ve, v0)
            emit("scalar", lambda e: e.activation(out=st_c(T0), in_=st_c(VAR), func=AF.Ln))
            a0 = emit("scalar", lambda e: e.activation(out=st_c(INV), in_=st_c(T0), func=AF.Exp,
                                                       scale=-0.5), inc=ac)
            wait("vector", ac, a0)
            emit("vector", lambda e: e.tensor_mul(st_c(T1), st_c(INV), st_c(INV)))
            emit("vector", lambda e: e.tensor_mul(st_c(T1), st_c(T1), st_c(VAR)))
            emit("vector", lambda e: e.tensor_scalar(st_c(T1), st_c(T1), -0.5, 1.5, ALU.mult, ALU.add))
            emit("vector", lambda e: e.tensor_mul(st_c(INV), st_c(INV), st_c(T1)))
            emit("vector", lambda e: e.tensor_mul(st_c(LNB), st_c(MEAN), st_c(INV)))
            return emit("vector", lambda e: e.tensor_scalar_mul(st_c(LNB), st_c(LNB), -1.0), inc=ve)

        v_inv1 = batch_invstd()

        wait("vector", ve, v_inv1)
        wait("vector", ac, AC_STAGE1)
        wait("vector", ve, VE_STAGE1)
        for t in range(TL):
            emit("vector", lambda e, _t=t: e.tensor_scalar(
                bufB[:, _t * 2048:(_t + 1) * 2048], bufB[:, _t * 2048:(_t + 1) * 2048],
                st1(INV, _t), st1(LNB, _t), ALU.mult, ALU.add), inc=ve)
        VE_APPLY1 = ve.n

        def transpose_tm_to_fm(src_buf, dst_buf, slabs):
            slab_rel = [None, None]
            for j in range(TD):
                sl = j % 2
                if slab_rel[sl] is not None:
                    wait("tensor", ve, slab_rel[sl])

                def tgrp(e, _j=j, _sl=sl):
                    ins = None
                    for t_ in range(TL):
                        ins = e.transpose(out=slabs[_sl][:, t_ * 128:(t_ + 1) * 128],
                                          in_=src_buf[:, t_ * 2048 + _j * 128:
                                                      t_ * 2048 + _j * 128 + 128],
                                          identity=ident[:])
                    return ins
                vte = emit("tensor", tgrp, inc=te)
                wait("vector", te, vte)
                slab_rel[sl] = emit("vector", lambda e, _j=j, _sl=sl: e.tensor_copy(
                    out=dst_buf[:, _j * 2048:(_j + 1) * 2048], in_=slabs[_sl][:]), inc=ve)

        wait("tensor", ve, ve.n)
        wait("tensor", ac, ac.n)
        with nc.psum_tensor("slA0", [128, 2048], BF16) as slA0, \
             nc.psum_tensor("slA1", [128, 2048], BF16) as slA1:
            wait("tensor", miscs, MISC_S)
            wait("tensor", ve, VE_APPLY1)
            transpose_tm_to_fm(bufB, bufA, [slA0, slA1])
        TE_A = te.n
        VE_A = ve.n

        natsl = [wsc[:, 0:2048], wsc[:, 2048:4096]]
        tsl = [wsc[:, 4096:6144], wsc[:, 6144:8192]]
        gsqb = [gvw[:, 0:2048], gvw[:, 2048:4096]]

        def rowblock_into(stream, dram, m, dst):
            return stream.load(lambda e, _m=m, _d=dst: e.dma_start(
                out=_d, in_=dram[_m * 128:(_m + 1) * 128, :]))

        wait("tensor", ve, ve.n)
        wait("tensor", ac, ac.n)
        with nc.psum_tensor("pg0", [128, 1024], F32) as pg0, \
             nc.psum_tensor("pg1", [128, 1024], F32) as pg1, \
             nc.psum_tensor("slW", [128, 2048], BF16) as slW, \
             nc.psum_tensor("slQ", [128, 2048], BF16) as slQ:
            pgs = [pg0, pg1]
            pg_rel = [None, None]
            slW_rel = None
            slQ_rel = None
            sq_ev = {}

            wg_slots = {0: rowblock_into(wgs, wg_d, 0, natsl[0])}
            copy_ev = {}
            mm_last = {}

            def wg_prep(m):
                slot = wg_slots[m]
                use("tensor", slot)
                nonlocal slW_rel
                if slW_rel is not None:
                    wait("tensor", ve, slW_rel)

                def wtr(e, _i=slot[0]):
                    ins = None
                    for k in range(TD):
                        ins = e.transpose(out=slW[:, k * 128:(k + 1) * 128],
                                          in_=natsl[_i][:, k * 128:(k + 1) * 128],
                                          identity=ident[:])
                    return ins
                vte = emit("tensor", wtr, inc=te)
                wait("vector", te, vte)
                if m >= 2 and (m - 2) in mm_last:
                    wait("vector", te, mm_last[m - 2])
                slW_rel = emit("vector", lambda e, _m=m: e.tensor_copy(
                    out=tsl[_m % 2][:], in_=slW[:]), inc=ve)
                copy_ev[m] = slW_rel
                wgs.release(slot, [(te, vte)])

            wg_prep(0)
            for m in range(TEm):
                if m + 1 < TEm:
                    wg_slots[m + 1] = rowblock_into(wgs, wg_d, m + 1, natsl[(m + 1) % 2])
                if m % 2 == 0:
                    j = m // 2
                    xq0.n += 16
                    on("gpsimd", lambda e, _j=j, _h=xq0.h: e.dma_start(
                        out=xbf_d[_j * 256:(_j + 1) * 256, :],
                        in_=xin[_j * 256:(_j + 1) * 256, :]).then_inc(_h, 16))
                wait("tensor", ve, copy_ev[m])
                for h in range(2):
                    if pg_rel[h] is not None:
                        wait("tensor", ac, pg_rel[h])

                    def gmm(e, _m=m, _h=h):
                        ins = None
                        for k in range(TD):
                            lhsT = tsl[_m % 2][:, k * 128:(k + 1) * 128]
                            for n2 in range(2):
                                nn = _h * 1024 + n2 * 512
                                ins = e.matmul(out=pgs[_h][:, n2 * 512:(n2 + 1) * 512],
                                               lhsT=lhsT,
                                               rhs=bufA[:, k * 2048 + nn:k * 2048 + nn + 512],
                                               start=(k == 0), stop=(k == TD - 1))
                        return ins
                    vmm = emit("tensor", gmm, inc=te)
                    mm_last[m] = vmm
                    wait("scalar", te, vmm)
                    wait("scalar", miscg, MISC_G)
                    emit("scalar", lambda e, _m=m, _h=h: e.activation(
                        out=bufB[:, _m * 2048 + _h * 1024:_m * 2048 + _h * 1024 + 1024],
                        in_=pgs[_h][:], func=AF.Copy, scale=gwt[:, _m:_m + 1]), inc=ac)
                    pg_rel[h] = emit("scalar", lambda e, _m=m, _h=h: e.activation(
                        out=gsqb[_m % 2][:, _h * 1024:_h * 1024 + 1024],
                        in_=pgs[_h][:], func=AF.Square), inc=ac)
                    if h == 0 and m + 1 < TEm:
                        wg_prep(m + 1)

                for mq in ([m - 1] if m >= 1 else []) + ([m] if m == TEm - 1 else []):
                    wait("tensor", ac, pg_rel[1] if mq == m else None) if mq == m else None
                    if mq == m:
                        wait("tensor", ac, pg_rel[1])
                    else:
                        wait("tensor", ac, sq_ev[mq])
                    if slQ_rel is not None:
                        wait("tensor", ve, slQ_rel)

                    def qtr(e, _m=mq):
                        ins = None
                        for t_ in range(TL):
                            ins = e.transpose(out=slQ[:, t_ * 128:(t_ + 1) * 128],
                                              in_=gsqb[_m % 2][:, t_ * 128:(t_ + 1) * 128],
                                              identity=ident[:])
                        return ins
                    vq = emit("tensor", qtr, inc=te)
                    wait("vector", te, vq)
                    if mq == 0:
                        slQ_rel = emit("vector", lambda e: e.tensor_reduce(
                            out=st_c(SS), in_=slQ[:].rearrange("p (t c) -> p t c", t=TL),
                            axis=AX.X, op=ALU.add), inc=ve)
                    else:
                        emit("vector", lambda e: e.tensor_reduce(
                            out=st_c(T0), in_=slQ[:].rearrange("p (t c) -> p t c", t=TL),
                            axis=AX.X, op=ALU.add))
                        slQ_rel = emit("vector", lambda e: e.tensor_add(
                            st_c(SS), st_c(SS), st_c(T0)), inc=ve)
                sq_ev[m] = pg_rel[1]

            v0 = emit("vector", lambda e: e.tensor_scalar(
                st_c(VAR), st_c(SS), 1.0 / EM, EPS, ALU.mult, ALU.add), inc=ve)
            wait("scalar", ve, v0)
            emit("scalar", lambda e: e.activation(out=st_c(T0), in_=st_c(VAR), func=AF.Ln))
            a0 = emit("scalar", lambda e: e.activation(out=st_c(RMS), in_=st_c(T0), func=AF.Exp,
                                                       scale=-0.5), inc=ac)
            wait("vector", ac, a0)
            emit("vector", lambda e: e.tensor_mul(st_c(T1), st_c(RMS), st_c(RMS)))
            emit("vector", lambda e: e.tensor_mul(st_c(T1), st_c(T1), st_c(VAR)))
            emit("vector", lambda e: e.tensor_scalar(st_c(T1), st_c(T1), -0.5, 1.5, ALU.mult, ALU.add))
            v_rms = emit("vector", lambda e: e.tensor_mul(st_c(RMS), st_c(RMS), st_c(T1)), inc=ve)
        TE_B0 = te.n

        winat = [gvw[:, 0:2048], gvw[:, 2048:4096]]
        wfnat = [gvw[:, 4096:6144], gvw[:, 6144:8192]]
        wiT = [gvw[:, 8192:10240], gvw[:, 10240:12288]]
        wfT = [gvw[:, 12288:14336], gvw[:, 14336:16384]]

        wait("gpsimd", te, TE_B0)
        wo_slots = {0: rowblock_into(wgs, wo_d, 0, natsl[0]),
                    1: rowblock_into(wgs, wo_d, 1, natsl[1])}

        wait("tensor", ve, ve.n)
        wait("tensor", ac, ac.n)
        with nc.psum_tensor("pi", [128, 1024], F32) as pi, \
             nc.psum_tensor("pf", [128, 1024], F32) as pf, \
             nc.psum_tensor("slWi", [128, 2048], BF16) as slWi, \
             nc.psum_tensor("slWf", [128, 2048], BF16) as slWf:
            slWi_rel = None
            slWf_rel = None
            pi_rel = None
            pf_rel = None
            wi_slots = {0: rowblock_into(wis, wi_d, 0, winat[0])}
            wf_slots = {0: rowblock_into(wfs, wf_d, 0, wfnat[0])}
            copy_evB = {}
            mm_lastB = {}
            u_ev = None

            def if_prep(m):
                nonlocal slWi_rel, slWf_rel
                for which in ("i", "f"):
                    slot = (wi_slots if which == "i" else wf_slots)[m]
                    nat = winat if which == "i" else wfnat
                    slab = slWi if which == "i" else slWf
                    tdst = wiT if which == "i" else wfT
                    rel = slWi_rel if which == "i" else slWf_rel
                    use("tensor", slot)
                    if rel is not None:
                        wait("tensor", ve, rel)

                    def wtr(e, _i=slot[0], _nat=nat, _slab=slab):
                        ins = None
                        for k in range(TD):
                            ins = e.transpose(out=_slab[:, k * 128:(k + 1) * 128],
                                              in_=_nat[_i][:, k * 128:(k + 1) * 128],
                                              identity=ident[:])
                        return ins
                    vte = emit("tensor", wtr, inc=te)
                    wait("vector", te, vte)
                    if m >= 2 and (m - 2) in mm_lastB:
                        wait("vector", te, mm_lastB[m - 2])
                    nrel = emit("vector", lambda e, _m=m, _slab=slab, _tdst=tdst: e.tensor_copy(
                        out=_tdst[_m % 2][:], in_=_slab[:]), inc=ve)
                    if which == "i":
                        slWi_rel = nrel
                    else:
                        slWf_rel = nrel
                    (wis if which == "i" else wfs).release(slot, [(te, vte)])
                copy_evB[m] = nrel

            if_prep(0)
            for m in range(TEm):
                if m + 1 < TEm:
                    wi_slots[m + 1] = rowblock_into(wis, wi_d, m + 1, winat[(m + 1) % 2])
                    wf_slots[m + 1] = rowblock_into(wfs, wf_d, m + 1, wfnat[(m + 1) % 2])
                wait("tensor", ve, copy_evB[m])
                for h in range(2):
                    for which in ("i", "f"):
                        ps_ = pi if which == "i" else pf
                        wT_ = wiT if which == "i" else wfT
                        relv = pi_rel if which == "i" else pf_rel
                        if relv is not None:
                            if which == "i":
                                wait("tensor", ve, relv)
                            else:
                                wait("tensor", ac, relv)

                        def mmgrp(e, _m=m, _h=h, _ps=ps_, _wT=wT_):
                            ins = None
                            for k in range(TD):
                                lhsT = _wT[_m % 2][:, k * 128:(k + 1) * 128]
                                for n2 in range(2):
                                    nn = _h * 1024 + n2 * 512
                                    ins = e.matmul(out=_ps[:, n2 * 512:(n2 + 1) * 512],
                                                   lhsT=lhsT,
                                                   rhs=bufA[:, k * 2048 + nn:k * 2048 + nn + 512],
                                                   start=(k == 0), stop=(k == TD - 1))
                            return ins
                        vmm = emit("tensor", mmgrp, inc=te)
                        mm_lastB[m] = vmm
                        wait("scalar", te, vmm)
                        if u_ev is not None:
                            wait("scalar", ve, u_ev)
                        if which == "i":
                            a1 = emit("scalar", lambda e, _h=h: e.activation(
                                out=bX[:, _h * 1024:_h * 1024 + 1024], in_=pi[:],
                                func=AF.Sigmoid), inc=ac)
                            wait("vector", ac, a1)
                            pi_rel = emit("vector", lambda e, _h=h: e.tensor_tensor(
                                out=bX[:, _h * 1024:_h * 1024 + 1024],
                                in0=bX[:, _h * 1024:_h * 1024 + 1024],
                                in1=pi[:], op=ALU.mult), inc=ve)
                        else:
                            pf_rel = emit("scalar", lambda e, _h=h: e.activation(
                                out=bGate[:, _h * 1024:_h * 1024 + 1024], in_=pf[:],
                                func=AF.Sigmoid), inc=ac)
                            wait("vector", ac, pf_rel)
                            emit("vector", lambda e, _h=h: e.scalar_tensor_tensor(
                                out=bX[:, _h * 1024:_h * 1024 + 1024],
                                in0=bGate[:, _h * 1024:_h * 1024 + 1024], scalar=1.0,
                                in1=bX[:, _h * 1024:_h * 1024 + 1024],
                                op0=ALU.subtract, op1=ALU.mult))
                    if h == 0 and m + 1 < TEm:
                        if_prep(m + 1)

                v_scan = emit("vector", lambda e: e.tensor_tensor_scan(
                    out=bX, data0=bGate, data1=bX, initial=0.0,
                    op0=ALU.mult, op1=ALU.subtract), inc=ve)
                wait("scalar", ve, v_scan)
                a_sw = emit("scalar", lambda e: e.activation(
                    out=bGate, in_=bX, func=AF.Sigmoid), inc=ac)
                wait("vector", ac, a_sw)
                emit("vector", lambda e: e.tensor_tensor(out=bX, in0=bX, in1=bGate, op=ALU.mult))
                u_ev = emit("vector", lambda e, _m=m: e.tensor_tensor(
                    out=bufB[:, _m * 2048:(_m + 1) * 2048], in0=bX,
                    in1=bufB[:, _m * 2048:(_m + 1) * 2048], op=ALU.mult), inc=ve)
        TE_B = te.n
        VE_B_PSUM = pi_rel
        AC_B_PSUM = pf_rel
        xs.rel[0] = [(ve, ve.n)]
        xs.rel[1] = [(ve, ve.n)]

        wait("tensor", ve, VE_B_PSUM)
        wait("tensor", ac, AC_B_PSUM)
        with nc.psum_tensor("po0", [128, 1024], F32) as po0, \
             nc.psum_tensor("po1", [128, 1024], F32) as po1, \
             nc.psum_tensor("slWo", [128, 2048], BF16) as slWo:
            pos = [po0, po1]
            po_rel = [None, None]
            slWo_rel = None
            copy_evO = {}
            mm_lastO = {}

            def wo_prep(m):
                nonlocal slWo_rel
                slot = wo_slots[m]
                use("tensor", slot)
                if slWo_rel is not None:
                    wait("tensor", ve, slWo_rel)

                def wtr(e, _i=slot[0]):
                    ins = None
                    for k in range(TD):
                        ins = e.transpose(out=slWo[:, k * 128:(k + 1) * 128],
                                          in_=natsl[_i][:, k * 128:(k + 1) * 128],
                                          identity=ident[:])
                    return ins
                vte = emit("tensor", wtr, inc=te)
                wait("vector", te, vte)
                if m >= 2 and (m - 2) in mm_lastO:
                    wait("vector", te, mm_lastO[m - 2])
                else:
                    wait("vector", te, TE_B0)
                slWo_rel = emit("vector", lambda e, _m=m: e.tensor_copy(
                    out=tsl[_m % 2][:], in_=slWo[:]), inc=ve)
                copy_evO[m] = slWo_rel
                wgs.release(slot, [(te, vte)])

            wo_prep(0)
            for m in range(TD):
                if m + 1 < TD and (m + 1) not in wo_slots:
                    wo_slots[m + 1] = rowblock_into(wgs, wo_d, m + 1, natsl[(m + 1) % 2])
                wait("tensor", ve, copy_evO[m])
                for h in range(2):
                    if po_rel[h] is not None:
                        wait("tensor", ve, po_rel[h])

                    def ommgrp(e, _m=m, _h=h):
                        ins = None
                        for k in range(TEm):
                            lhsT = tsl[_m % 2][:, k * 128:(k + 1) * 128]
                            for n2 in range(2):
                                nn = _h * 1024 + n2 * 512
                                ins = e.matmul(out=pos[_h][:, n2 * 512:(n2 + 1) * 512],
                                               lhsT=lhsT,
                                               rhs=bufB[:, k * 2048 + nn:k * 2048 + nn + 512],
                                               start=(k == 0), stop=(k == TEm - 1))
                        return ins
                    vmm = emit("tensor", ommgrp, inc=te)
                    mm_lastO[m] = vmm
                    wait("vector", te, vmm)
                    po_rel[h] = emit("vector", lambda e, _m=m, _h=h: e.tensor_copy(
                        out=bufA[:, _m * 2048 + _h * 1024:_m * 2048 + _h * 1024 + 1024],
                        in_=pos[_h][:]), inc=ve)
                    if h == 0 and m + 1 < TD:
                        wo_prep(m + 1)
        VE_ATTN = ve.n
        TE_B3 = te.n

        wait("tensor", ve, ve.n)
        wait("tensor", ac, ac.n)
        with nc.psum_tensor("slT0", [128, 2048], BF16) as slT0, \
             nc.psum_tensor("slT1", [128, 2048], BF16) as slT1:
            slTs = [slT0, slT1]
            slT_rel = [None, None]
            abuf = [gvw[:, 2048:4096], gvw[:, 4096:6144]]
            wait("gpsimd", te, TE_B3)
            wait("gpsimd", xq0, xq0.n)
            xch = [S(f"xch{k}") for k in range(4)]
            for k in range(4):
                xch[k].n += 16
                on("gpsimd", lambda e, _k=k, _h=xch[k].h: e.dma_start(
                    out=bufB[:, _k * 4 * 2048:(_k + 1) * 4 * 2048].rearrange(
                        "p (t d) -> p t d", t=4),
                    in_=xbf_d[_k * 4 * 128:(_k + 1) * 4 * 128, :].rearrange(
                        "(t p) d -> p t d", p=128)).then_inc(_h, 16))
            for t in range(TL):
                sl = t % 2
                wait("tensor", ve, VE_ATTN)
                if slT_rel[sl] is not None:
                    wait("tensor", ve, slT_rel[sl])

                def tgrp(e, _t=t, _sl=sl):
                    ins = None
                    for j in range(TD):
                        ins = e.transpose(out=slTs[_sl][:, j * 128:(j + 1) * 128],
                                          in_=bufA[:, j * 2048 + _t * 128:j * 2048 + _t * 128 + 128],
                                          identity=ident[:])
                    return ins
                vte = emit("tensor", tgrp, inc=te)
                i = t % 2
                if xs.rel[i]:
                    for (rs, rv) in xs.rel[i]:
                        wait("vector", rs, rv)
                    xs.rel[i] = None
                wait("vector", te, vte)
                wait("vector", ve, v_rms)
                v_r = emit("vector", lambda e, _i=i, _t=t, _sl=sl: e.tensor_scalar_mul(
                    abuf[_i], slTs[_sl][:], st1(RMS, _t)), inc=ve)
                slT_rel[sl] = v_r
                wait("gpsimd", ve, v_r)
                wait("gpsimd", ypre, 16)
                spl = spill.load(lambda e, _i=i, _t=t: e.dma_start(
                    out=yout[_t * 128:(_t + 1) * 128, :], in_=abuf[_i],
                    accum_op=ALU.add))
                wait("vector", xch[t // 4], 16)
                v_cp = emit("vector", lambda e, _i=i, _t=t: e.scalar_tensor_tensor(
                    out=bufB[:, _t * 2048:(_t + 1) * 2048], in0=abuf[_i], scalar=1.0,
                    in1=bufB[:, _t * 2048:(_t + 1) * 2048],
                    op0=ALU.mult, op1=ALU.add, accum_out=st1(SUM, _t)), inc=ve)
                wait("scalar", ve, v_cp)
                emit("scalar", lambda e, _t=t: e.activation(
                    out=gvw[:, 0:2048], in_=bufB[:, _t * 2048:(_t + 1) * 2048],
                    func=AF.Square, accum_out=st1(SQ, _t)), inc=ac)
                xs.rel[i] = [(ve, v_cp), (spl[1], spl[2])]

        v_inv2 = batch_invstd()

        wait("vector", ve, v_inv2)
        wait("vector", ac, ac.n)
        for t in range(TL):
            emit("vector", lambda e, _t=t: e.tensor_scalar(
                bufB[:, _t * 2048:(_t + 1) * 2048], bufB[:, _t * 2048:(_t + 1) * 2048],
                st1(INV, _t), st1(LNB, _t), ALU.mult, ALU.add), inc=ve)
        VE_APPLY2 = ve.n

        wait("tensor", ve, ve.n)
        wait("tensor", ac, ac.n)
        with nc.psum_tensor("slC0", [128, 2048], BF16) as slC0, \
             nc.psum_tensor("slC1", [128, 2048], BF16) as slC1:
            wait("tensor", ve, VE_APPLY2)
            transpose_tm_to_fm(bufB, bufA, [slC0, slC1])

        rwbf = gvw[0:8, 0:2048]
        for s_ in spill.sems:
            wait("vector", s_)
            wait("scalar", s_)
        wait("vector", ac, ac.n)
        wait("scalar", ve, ve.n)
        wait("gpsimd", te, te.n)
        m_rw = misc_load("gpsimd", lambda e: e.dma_start(out=rwbf, in_=rw_d))
        wait("tensor", ve, ve.n)
        wait("tensor", ac, ac.n)
        with nc.psum_tensor("psRT", [128, 128], BF16) as psRT, \
             nc.psum_tensor("psL", [128, 2048], F32) as psL, \
             nc.psum_tensor("psLT", [128, 128], F32) as psLT, \
             nc.psum_tensor("psD", [128, 2048], BF16) as psD:
            wait("tensor", miscg, m_rw)

            def rtr(e):
                ins = None
                for k in range(TD):
                    ins = e.transpose(out=psRT[:, k * 8:(k + 1) * 8],
                                      in_=rwbf[:, k * 128:(k + 1) * 128],
                                      identity=ident[0:8, 0:8])
                return ins
            vte = emit("tensor", rtr, inc=te)
            wait("vector", te, vte)
            v_rt = emit("vector", lambda e: e.tensor_copy(out=routerT[:], in_=psRT[:]), inc=ve)

            wait("tensor", ve, v_rt)

            def lmm(e):
                ins = None
                for n in range(4):
                    for k in range(TD):
                        ins = e.matmul(out=psL[0:8, n * 512:(n + 1) * 512],
                                       lhsT=routerT[:, k * 8:(k + 1) * 8],
                                       rhs=bufA[:, k * 2048 + n * 512:k * 2048 + n * 512 + 512],
                                       start=(k == 0), stop=(k == TD - 1))
                return ins
            v_lg = emit("tensor", lmm, inc=te)
            wait("vector", te, v_lg)
            logf = fsc[0:8, 0:2048]
            v_lf = emit("vector", lambda e: e.tensor_copy(out=logf, in_=psL[0:8, :]), inc=ve)
            wait("tensor", ve, v_lf)

            def ltr(e):
                ins = None
                for m in range(TL):
                    ins = e.transpose(out=psLT[:, m * 8:(m + 1) * 8],
                                      in_=logf[0:8, m * 128:(m + 1) * 128],
                                      identity=identf8[:])
                return ins
            v_lt = emit("tensor", ltr, inc=te)
            wait("vector", te, v_lt)
            emit("vector", lambda e: e.tensor_copy(out=logtm[:], in_=psLT[:]))

            for m in range(TL):
                emit("vector", lambda e, _m=m: e.max(
                    out=top8s[:, _m * 8:(_m + 1) * 8], in_=logtm[:, _m * 8:(_m + 1) * 8]))
            t8v = top8s[:].rearrange("p (m c) -> p m c", m=TL)
            v_da = emit("vector", lambda e: e.tensor_tensor(
                out=st_c(DALL), in0=t8v[:, :, 1:2], in1=t8v[:, :, 0:1],
                op=ALU.subtract), inc=ve)
            wait("scalar", ve, v_da)
            a_w2 = emit("scalar", lambda e: e.activation(
                out=st_c(WAcol), in_=st_c(DALL), func=AF.Sigmoid), inc=ac)
            wait("vector", ac, a_w2)
            emit("vector", lambda e: e.tensor_scalar(
                st_c(WBcol), st_c(WAcol), -1.0, 1.0, ALU.mult, ALU.add))
            for m in range(TL):
                lg_m = logtm[:, m * 8:(m + 1) * 8]
                emit("vector", lambda e, _l=lg_m, _m=m: e.tensor_scalar(
                    tmp8a[:], _l, top8s[:, _m * 8:_m * 8 + 1], st1(WBcol, _m),
                    ALU.is_equal, ALU.mult))
                emit("vector", lambda e, _l=lg_m, _m=m: e.tensor_scalar(
                    tmp8b[:], _l, top8s[:, _m * 8 + 1:_m * 8 + 2], st1(WAcol, _m),
                    ALU.is_equal, ALU.mult))
                emit("vector", lambda e, _m=m: e.tensor_add(
                    wdense[:, _m * 8:(_m + 1) * 8], tmp8a[:], tmp8b[:]))
            v_wd = emit("vector", lambda e: e.tensor_scalar_add(tmp8a[:], tmp8a[:], 0.0), inc=ve)

            wait("tensor", ve, v_wd)

            def wdtr(e):
                ins = None
                for m in range(TL):
                    ins = e.transpose(out=psD[0:8, m * 128:(m + 1) * 128],
                                      in_=wdense[:, m * 8:(m + 1) * 8],
                                      identity=ident[:])
                return ins
            v_wt = emit("tensor", wdtr, inc=te)
            wait("vector", te, v_wt)
            v_wfm = emit("vector", lambda e: e.tensor_copy(out=wfm[:], in_=psD[0:8, :]), inc=ve)

        c1sl = [wsc[:, 0:2048], wsc[:, 2048:4096]]
        c2sl = [wsc[:, 4096:6144], wsc[:, 6144:8192]]
        c3sl = [wsc[:, 8192:9216], wsc[:, 9216:10240]]
        vs_scr = scr2[:, 0:1024]

        wait("gpsimd", te, TE_B3)

        def col_load(stream, dram, e_, blk, dst, ktiles):
            return stream.load(lambda eng, _e=e_, _b=blk, _d=dst, _k=ktiles: eng.dma_start(
                out=_d.rearrange("p (k c) -> p k c", k=_k),
                in_=dram[_e, :, _b * 128:(_b + 1) * 128].rearrange("(k p) c -> p k c", p=128)))

        wait("tensor", ve, ve.n)
        wait("tensor", ac, ac.n)
        with nc.psum_tensor("pga", [128, 1024], F32) as pga, \
             nc.psum_tensor("pva", [128, 1024], F32) as pva, \
             nc.psum_tensor("pw0", [128, 1024], F32) as pw0, \
             nc.psum_tensor("pw1", [128, 1024], F32) as pw1:
            pws = [pw0, pw1]
            pga_rel = None
            pva_rel = None
            pw_rel = [None, None]
            last_gv = None
            gv_ev7 = {}

            c1q = {(0, 0): col_load(wc1, w1_d, 0, 0, c1sl[0], 16)}
            c2q = {(0, 0): col_load(wc2, w2_d, 0, 0, c2sl[0], 16)}
            c3q = {}

            for e_ in range(NEXP):
                wait("tensor", miscs, MISC_S)
                wait("tensor", ve, v_wfm)
                for hb in range(2):
                    if pw_rel[hb] is not None:
                        wait("tensor", ve, pw_rel[hb])

                    def bc(e, _e=e_, _hb=hb):
                        ins = None
                        for n2 in range(2):
                            ins = e.matmul(out=pws[_hb][:, n2 * 512:(n2 + 1) * 512],
                                           lhsT=selsb[:, _e * 128:(_e + 1) * 128],
                                           rhs=wfm[0:8,
                                                   _hb * 1024 + n2 * 512:_hb * 1024 + (n2 + 1) * 512],
                                           start=True, stop=True)
                        return ins
                    v_b = emit("tensor", bc, inc=te)
                    wait("vector", te, v_b)
                    pw_rel[hb] = emit("vector", lambda e, _hb=hb: e.tensor_copy(
                        out=wb_ap[:, _hb * 1024:_hb * 1024 + 1024], in_=pws[_hb][:]), inc=ve)
                v_wb = pw_rel[1]

                for fm in range(TFm):
                    nb = fm + 1
                    if nb < TFm:
                        c1q[(e_, nb)] = col_load(wc1, w1_d, e_, nb, c1sl[nb % 2], 16)
                        c2q[(e_, nb)] = col_load(wc2, w2_d, e_, nb, c2sl[nb % 2], 16)
                    elif e_ + 1 < NEXP:
                        c1q[(e_ + 1, 0)] = col_load(wc1, w1_d, e_ + 1, 0, c1sl[nb % 2], 16)
                        c2q[(e_ + 1, 0)] = col_load(wc2, w2_d, e_ + 1, 0, c2sl[nb % 2], 16)
                    s1 = c1q[(e_, fm)]
                    s2 = c2q[(e_, fm)]
                    use("tensor", s1)
                    use("tensor", s2)
                    v_g = v_v = None
                    for h in range(2):
                        if pga_rel is not None:
                            wait("tensor", ve, pga_rel)

                        def g1(e, _h=h, _i=s1[0]):
                            ins = None
                            for k in range(TD):
                                lhsT = c1sl[_i][:, k * 128:(k + 1) * 128]
                                for n2 in range(2):
                                    nn = _h * 1024 + n2 * 512
                                    ins = e.matmul(out=pga[:, n2 * 512:(n2 + 1) * 512],
                                                   lhsT=lhsT,
                                                   rhs=bufA[:, k * 2048 + nn:k * 2048 + nn + 512],
                                                   start=(k == 0), stop=(k == TD - 1))
                            return ins
                        v_g = emit("tensor", g1, inc=te)
                        if pva_rel is not None:
                            wait("tensor", ve, pva_rel)

                        def g2(e, _h=h, _i=s2[0]):
                            ins = None
                            for k in range(TD):
                                lhsT = c2sl[_i][:, k * 128:(k + 1) * 128]
                                for n2 in range(2):
                                    nn = _h * 1024 + n2 * 512
                                    ins = e.matmul(out=pva[:, n2 * 512:(n2 + 1) * 512],
                                                   lhsT=lhsT,
                                                   rhs=bufA[:, k * 2048 + nn:k * 2048 + nn + 512],
                                                   start=(k == 0), stop=(k == TD - 1))
                            return ins
                        v_v = emit("tensor", g2, inc=te)

                        wait("scalar", te, v_g)
                        a_sg = emit("scalar", lambda e: e.activation(
                            out=sg_ap, in_=pga[:], func=AF.Sigmoid), inc=ac)
                        wait("vector", ac, a_sg)
                        gv_sl = gvw[:, fm * 2048 + h * 1024:fm * 2048 + h * 1024 + 1024]
                        pga_rel = emit("vector", lambda e, _g=gv_sl: e.tensor_tensor(
                            out=_g, in0=sg_ap, in1=pga[:], op=ALU.mult), inc=ve)
                        wait("vector", te, v_v)
                        wait("vector", ve, v_wb)
                        pva_rel = emit("vector", lambda e, _h=h: e.tensor_tensor(
                            out=vs_scr, in0=pva[:], in1=wb_ap[:, _h * 1024:_h * 1024 + 1024],
                            op=ALU.mult), inc=ve)
                        last_gv = emit("vector", lambda e, _g=gv_sl: e.tensor_tensor(
                            out=_g, in0=_g, in1=vs_scr, op=ALU.mult), inc=ve)
                        gv_ev7[fm] = last_gv
                    wc1.release(s1, [(te, v_g)])
                    wc2.release(s2, [(te, v_v)])

                for mD in range(TD):
                    if (e_, mD) not in c3q:
                        c3q[(e_, mD)] = col_load(wc3, w3_d, e_, mD, c3sl[mD % 2], KW3)
                    if mD + 1 < TD:
                        c3q[(e_, mD + 1)] = col_load(wc3, w3_d, e_, mD + 1, c3sl[(mD + 1) % 2], KW3)
                    elif e_ + 1 < NEXP:
                        c3q[(e_ + 1, 0)] = col_load(wc3, w3_d, e_ + 1, 0, c3sl[(mD + 1) % 2], KW3)
                    s3 = c3q[(e_, mD)]
                    use("tensor", s3)
                    if mD > 0:
                        wait("tensor", ve, last_gv)
                    v_w3 = None
                    for h in range(2):
                        sl = h
                        if pw_rel[sl] is not None:
                            wait("tensor", ve, pw_rel[sl])

                        if mD == 0:
                            wait("tensor", ve, gv_ev7[6])

                            def w3a(e, _h=h, _i=s3[0], _sl=sl):
                                ins = None
                                for k in range(KW3 - 1):
                                    lhsT = c3sl[_i][:, k * 128:(k + 1) * 128]
                                    for n2 in range(2):
                                        nn = _h * 1024 + n2 * 512
                                        ins = e.matmul(out=pws[_sl][:, n2 * 512:(n2 + 1) * 512],
                                                       lhsT=lhsT,
                                                       rhs=gvw[:, k * 2048 + nn:k * 2048 + nn + 512],
                                                       start=(k == 0), stop=False)
                                return ins
                            emit("tensor", w3a, inc=te)
                            wait("tensor", ve, last_gv)

                            def w3b(e, _h=h, _i=s3[0], _sl=sl):
                                ins = None
                                k = KW3 - 1
                                lhsT = c3sl[_i][:, k * 128:(k + 1) * 128]
                                for n2 in range(2):
                                    nn = _h * 1024 + n2 * 512
                                    ins = e.matmul(out=pws[_sl][:, n2 * 512:(n2 + 1) * 512],
                                                   lhsT=lhsT,
                                                   rhs=gvw[:, k * 2048 + nn:k * 2048 + nn + 512],
                                                   start=False, stop=True)
                                return ins
                            v_w3 = emit("tensor", w3b, inc=te)
                            continue_body = True
                        else:
                            def w3g(e, _mD=mD, _h=h, _i=s3[0], _sl=sl):
                                ins = None
                                for k in range(KW3):
                                    lhsT = c3sl[_i][:, k * 128:(k + 1) * 128]
                                    for n2 in range(2):
                                        nn = _h * 1024 + n2 * 512
                                        ins = e.matmul(out=pws[_sl][:, n2 * 512:(n2 + 1) * 512],
                                                       lhsT=lhsT,
                                                       rhs=gvw[:, k * 2048 + nn:k * 2048 + nn + 512],
                                                       start=(k == 0), stop=(k == KW3 - 1))
                                return ins
                            v_w3 = emit("tensor", w3g, inc=te)
                        wait("vector", te, v_w3)
                        dst = bufB[:, mD * 2048 + h * 1024:mD * 2048 + h * 1024 + 1024]
                        if e_ == 0:
                            pw_rel[sl] = emit("vector", lambda e, _d=dst, _sl=sl: e.tensor_copy(
                                out=_d, in_=pws[_sl][:]), inc=ve)
                        else:
                            pw_rel[sl] = emit("vector", lambda e, _d=dst, _sl=sl: e.tensor_tensor(
                                out=_d, in0=pws[_sl][:], in1=_d, op=ALU.add), inc=ve)
                    wc3.release(s3, [(te, v_w3)])
        VE_FFN = ve.n
        xs.rel[0] = [(ve, VE_FFN)]
        xs.rel[1] = [(ve, VE_FFN)]

        wait("tensor", ve, ve.n)
        wait("tensor", ac, ac.n)
        with nc.psum_tensor("slE0", [128, 2048], BF16) as slE0, \
             nc.psum_tensor("slE1", [128, 2048], BF16) as slE1:
            slEs = [slE0, slE1]
            slE_rel = [None, None]
            for s_ in spill.sems:
                wait("gpsimd", s_)
            ebuf = [gvw[:, k * 2048:(k + 1) * 2048] for k in range(4)]
            ebuf_rel = [None] * 4
            for t in range(TL):
                sl = t % 2
                wait("tensor", ve, VE_FFN)
                if slE_rel[sl] is not None:
                    wait("tensor", ve, slE_rel[sl])

                def tgrp(e, _t=t, _sl=sl):
                    ins = None
                    for j in range(TD):
                        ins = e.transpose(out=slEs[_sl][:, j * 128:(j + 1) * 128],
                                          in_=bufB[:, j * 2048 + _t * 128:j * 2048 + _t * 128 + 128],
                                          identity=ident[:])
                    return ins
                vte = emit("tensor", tgrp, inc=te)
                i = t % 4
                if ebuf_rel[i]:
                    for (rs, rv) in ebuf_rel[i]:
                        wait("vector", rs, rv)
                    ebuf_rel[i] = None
                wait("vector", te, vte)
                v_o = emit("vector", lambda e, _i=i, _sl=sl: e.tensor_copy(
                    out=ebuf[_i], in_=slEs[_sl][:]), inc=ve)
                slE_rel[sl] = v_o
                wait("gpsimd", ve, v_o)
                so = accs.load(lambda e, _i=i, _t=t: e.dma_start(
                    out=yout[_t * 128:(_t + 1) * 128, :], in_=ebuf[_i],
                    accum_op=ALU.add))
                ebuf_rel[i] = [(so[1], so[2])]

        for s_ in accs.sems:
            wait("gpsimd", s_)

        block = ctx.enter_context(nc.Block())

        @block.sync
        def _(eng):
            for f in plan["sync"]:
                f(eng)

        @block.gpsimd
        def _(eng):
            for f in plan["gpsimd"]:
                f(eng)

        @block.tensor
        def _(eng):
            for f in plan["tensor"]:
                f(eng)

        @block.vector
        def _(eng):
            for f in plan["vector"]:
                f(eng)

        @block.scalar
        def _(eng):
            for f in plan["scalar"]:
                f(eng)

    return nc


def kernel(L0_attn_hidden_states, L0_attn_Wi, L0_attn_Wf, L0_attn_Wg,
           L0_attn_g_norm_weight, L0_attn_Wo, L0_ffn_router_w,
           L0_ffn_w1, L0_ffn_w2, L0_ffn_w3):
    x = np.ascontiguousarray(np.asarray(L0_attn_hidden_states, dtype=np.float32))
    ws = {
        "wi": np.ascontiguousarray(np.asarray(L0_attn_Wi, dtype=np.float32)),
        "wf": np.ascontiguousarray(np.asarray(L0_attn_Wf, dtype=np.float32)),
        "wg": np.ascontiguousarray(np.asarray(L0_attn_Wg, dtype=np.float32)),
        "gwv": np.ascontiguousarray(np.asarray(L0_attn_g_norm_weight, dtype=np.float32)),
        "wo": np.ascontiguousarray(np.asarray(L0_attn_Wo, dtype=np.float32)),
        "rw": np.ascontiguousarray(np.asarray(L0_ffn_router_w, dtype=np.float32)),
        "w1": np.ascontiguousarray(np.asarray(L0_ffn_w1, dtype=np.float32)),
        "w2": np.ascontiguousarray(np.asarray(L0_ffn_w2, dtype=np.float32)),
        "w3": np.ascontiguousarray(np.asarray(L0_ffn_w3, dtype=np.float32)),
    }
    if "nc" not in _CACHED:
        _CACHED["nc"] = build()
    nc = _CACHED["nc"]
    in_maps = [dict(ws, x=x[i]) for i in range(8)]
    res = run_bass_kernel_spmd(nc, in_maps, core_ids=list(range(8)))
    out = np.stack([np.asarray(res.results[i]["y"], dtype=np.float32) for i in range(8)], axis=0)
    return out
